# version 02
# baseline (speedup 1.0000x reference)
import numpy as np
import ml_dtypes
from contextlib import ExitStack

import concourse.bass as bass
import concourse.mybir as mybir
from concourse.bass_utils import run_bass_kernel_spmd

BF = ml_dtypes.bfloat16
F32 = mybir.dt.float32
BF16 = mybir.dt.bfloat16
AF = mybir.ActivationFunctionType
ALU = mybir.AluOpType
AX = mybir.AxisListType

B, S, D = 4, 2048, 2048
NCORE = 8
NT = 8
TPC = NT * 128
NCH = D // 128
NEG = -1.0e30



def _tiles_of(parity):
    return [2 * ti + parity for ti in range(NT)]


def _positions(parity):
    return np.concatenate([np.arange(g * 128, (g + 1) * 128) for g in _tiles_of(parity)])


def _rope_tables(parity):
    pos = _positions(parity).astype(np.float64)
    j = np.arange(32, dtype=np.float64)
    inv = 10000.0 ** (-j / 32.0)
    fr = pos[None, :] * inv[:, None]
    fr4 = np.tile(fr, (4, 1))
    return np.cos(fr4).astype(np.float32), np.sin(fr4).astype(np.float32)


def _perm_qfeat():
    cols = []
    for c in range(8):
        cols += [64 * h + j for h in range(4 * c, 4 * c + 4) for j in range(32)]
        cols += [64 * h + 32 + j for h in range(4 * c, 4 * c + 4) for j in range(32)]
    return np.array(cols)


def _hadamard(n):
    h = np.array([[1.0]], np.float64)
    while h.shape[0] < n:
        h = np.block([[h, h], [h, -h]])
    return h



class Ctr:
    def __init__(self):
        self.v = {}

    def inc(self, sem, n=1):
        self.v[id(sem)] = self.v.get(id(sem), 0) + n
        return self.v[id(sem)]

    def val(self, sem):
        return self.v.get(id(sem), 0)


class Eng:

    def __init__(self, eng, tick, c):
        self._e, self._t, self._c = eng, tick, c

    def wait(self, sem, val):
        if val and val > 0:
            self._e.wait_ge(sem, val)

    def sync(self):
        v = self._c.val(self._t)
        if v > 0:
            self._e.wait_ge(self._t, v)

    def dma(self, out, in_, sem, ctr):
        self._e.dma_start(out=out, in_=in_).then_inc(sem, 16)
        ctr.inc(sem, 16)

    def __getattr__(self, name):
        fn = getattr(self._e, name)

        def call(*a, **k):
            ins = fn(*a, **k)
            ins.then_inc(self._t, 1)
            self._c.inc(self._t, 1)
            return ins
        return call


class Bld:

    ENGS = ("sync", "vector", "scalar", "tensor", "gpsimd")

    def __init__(self, nc, ctx):
        self.nc = nc
        self.c = Ctr()
        self.tick = {}
        for e in self.ENGS:
            self.tick[e] = ctx.enter_context(nc.semaphore("t_" + e))
        self.blk = ctx.enter_context(nc.Block())

    def sec(self, name, fn):
        def body(eng, fn=fn, name=name):
            fn(Eng(eng, self.tick[name], self.c))
        getattr(self.blk, name)(body)

    def t(self, name):
        return self.c.val(self.tick[name])

    def sem(self, ctx, name):
        return ctx.enter_context(self.nc.semaphore(name))



def build_launch1():
    nc = bass.Bass("TRN2", target_bir_lowering=False)
    dt = nc.dram_tensor
    x_i = dt("x", [NCH, 128, TPC], F32, kind="ExternalInput").ap()
    wa_i = dt("wa", [NCH, 128, 6144], F32, kind="ExternalInput").ap()
    wf_i = dt("wf", [NCH, 128, 2048], F32, kind="ExternalInput").ap()
    wg1_i = dt("wg1", [NCH, 128, 64], F32, kind="ExternalInput").ap()
    wg2_i = dt("wg2", [64, 2048], F32, kind="ExternalInput").ap()
    bias_i = dt("bias", [128, 16], F32, kind="ExternalInput").ap()
    cosq_i = dt("cosq", [128, TPC], F32, kind="ExternalInput").ap()
    sinq_i = dt("sinq", [128, TPC], F32, kind="ExternalInput").ap()
    cosk_i = dt("cosk", [128, TPC], F32, kind="ExternalInput").ap()
    sink_i = dt("sink", [128, TPC], F32, kind="ExternalInput").ap()

    qa_o = dt("qa", [8, 128, TPC], BF16, kind="ExternalOutput").ap()
    qb_o = dt("qb", [8, 128, TPC], BF16, kind="ExternalOutput").ap()
    ka_o = dt("ka", [8, 128, TPC], BF16, kind="ExternalOutput").ap()
    kb_o = dt("kb", [8, 128, TPC], BF16, kind="ExternalOutput").ap()
    v_o = dt("v", [NCH, 128, TPC], BF16, kind="ExternalOutput").ap()
    g_o = dt("g", [NCH, 128, TPC], BF16, kind="ExternalOutput").ap()
    pxa_o = dt("pxa", [NCH, 128, TPC], F32, kind="ExternalOutput").ap()

    with ExitStack() as ctx:
        sb = lambda n, s, d: ctx.enter_context(nc.sbuf_tensor(n, s, d))
        ps = lambda n, s: ctx.enter_context(nc.psum_tensor(n, s, F32))
        xs = sb("xs", [128, NCH, TPC], F32)
        xb = sb("xb", [128, NCH, TPC], BF16)
        mu = sb("mu", [1, TPC], F32)
        rstd = sb("rstd", [1, TPC], F32)
        nmr = sb("nmr", [1, TPC], F32)
        R1 = sb("R1", [128, TPC], BF16)
        R2 = sb("R2", [128, TPC], BF16)
        wl = sb("wl", [128, 2, NCH, 128], F32)
        wbf = sb("wbf", [128, 2, NCH, 128], BF16)
        awb = sb("awb", [128, NCH, 128], BF16)
        rt1 = sb("rt1", [128, TPC], F32)
        rt2 = sb("rt2", [128, TPC], F32)
        est = sb("est", [128, 4, TPC], BF16)
        tg = sb("tg", [64, TPC], BF16)
        wg1f = sb("wg1f", [128, NCH, 64], F32)
        wg2f = sb("wg2f", [64, 2048], F32)
        wg1b = sb("wg1b", [128, NCH, 64], BF16)
        wg2b = sb("wg2b", [64, 2048], BF16)
        alp = sb("alp", [128, 16], F32)
        biasc = sb("biasc", [128, 16], F32)
        cq = sb("cq", [128, TPC], F32)
        sq_ = sb("sq_", [128, TPC], F32)
        ck = sb("ck", [128, TPC], F32)
        sk = sb("sk", [128, TPC], F32)
        onesb = sb("onesb", [128, 1], BF16)
        ones1f = sb("ones1f", [1, 128], F32)
        pp0 = ps("pp0", [128, TPC])
        pp1 = ps("pp1", [128, TPC])
        pbc = ps("pbc", [128, TPC])
        pp = [pp0, pp1]

        bd = Bld(nc, ctx)
        c = bd.c
        sLX = bd.sem(ctx, "sLX")
        sLT = bd.sem(ctx, "sLT")
        sW0 = bd.sem(ctx, "sW0")
        sW1 = bd.sem(ctx, "sW1")
        sWp = [sW0, sW1]
        sT = [bd.sem(ctx, f"sT{i}") for i in range(4)]
        sTP = bd.sem(ctx, "sTP")

        def f(sy):
            for ch in range(NCH):
                sy.dma(xs[:, ch, :], x_i[ch], sLX, c)
            sy.dma(cq[:], cosq_i, sLT, c)
            sy.dma(sq_[:], sinq_i, sLT, c)
            sy.dma(ck[:], cosk_i, sLT, c)
            sy.dma(sk[:], sink_i, sLT, c)
            sy.dma(biasc[:], bias_i, sLT, c)
            for ch in range(NCH):
                sy.dma(wg1f[:, ch, :], wg1_i[ch], sLT, c)
            sy.dma(wg2f[:], wg2_i, sLT, c)
        bd.sec("sync", f)
        lX, lT = c.val(sLX), c.val(sLT)

        def f(gp):
            gp.memset(onesb[:], 1.0)
            gp.memset(ones1f[:], 1.0)
        bd.sec("gpsimd", f)
        tOnes = bd.t("gpsimd")

        def f(dv):
            dv.wait(sLX, lX)
            dv.wait(sLT, lT)
            for ch in range(NCH):
                dv.tensor_copy(out=xb[:, ch, :], in_=xs[:, ch, :])
        bd.sec("vector", f)
        tXB = bd.t("vector")

        peRead = []
        for ch in range(NCH):
            def fa(ac, ch=ch):
                if ch == 0:
                    ac.wait(bd.tick["vector"], tXB)
                if ch >= 2:
                    ac.wait(bd.tick["tensor"], peRead[ch - 2])
                ac.activation(out=est[:, ch % 2, :], in_=xb[:, ch, :], func=AF.Square)
            bd.sec("scalar", fa)
            sqd = bd.t("scalar")

            def ft(te, ch=ch, sqd=sqd):
                if ch == 0:
                    te.wait(bd.tick["vector"], tXB)
                    te.wait(bd.tick["gpsimd"], tOnes)
                stt = (ch == 0)
                sp = (ch == NCH - 1)
                for h in range(2):
                    te.matmul(pbc[0:1, h * 512:(h + 1) * 512], onesb[:],
                              xb[:, ch, h * 512:(h + 1) * 512], start=stt, stop=sp)
                te.wait(bd.tick["scalar"], sqd)
                for h in range(2):
                    te.matmul(pp0[0:1, h * 512:(h + 1) * 512], onesb[:],
                              est[:, ch % 2, h * 512:(h + 1) * 512], start=stt, stop=sp)
            bd.sec("tensor", ft)
            peRead.append(bd.t("tensor"))
        tStats = bd.t("tensor")

        def f(dv):
            dv.wait(bd.tick["tensor"], tStats)
            dv.tensor_scalar(out=mu[:], in0=pbc[0:1, 0:TPC], scalar1=1.0 / D,
                             scalar2=None, op0=ALU.mult)
            dv.tensor_scalar(out=rstd[:], in0=pp0[0:1, 0:TPC], scalar1=1.0 / D,
                             scalar2=1e-5, op0=ALU.mult, op1=ALU.add)
            dv.sync()
            dv.tensor_mul(out=nmr[:], in0=mu[:], in1=mu[:])
            dv.sync()
            dv.tensor_sub(out=rstd[:], in0=rstd[:], in1=nmr[:])
        bd.sec("vector", f)
        t1 = bd.t("vector")

        def f(ac):
            ac.wait(bd.tick["vector"], t1)
            ac.activation(out=rstd[:], in_=rstd[:], func=AF.Sqrt)
        bd.sec("scalar", f)
        t2 = bd.t("scalar")

        def f(dv):
            dv.wait(bd.tick["scalar"], t2)
            dv.reciprocal(out=rstd[:], in_=rstd[:])
            dv.sync()
            dv.scalar_tensor_tensor(out=nmr[:], in0=mu[:], scalar=-1.0, in1=rstd[:],
                                    op0=ALU.mult, op1=ALU.mult)
        bd.sec("vector", f)
        t3 = bd.t("vector")

        def f(te):
            te.wait(bd.tick["vector"], t3)
            for h in range(2):
                te.matmul(pbc[:, h * 512:(h + 1) * 512], ones1f[:],
                          rstd[0:1, h * 512:(h + 1) * 512], start=True, stop=True)
            for h in range(2):
                te.matmul(pp0[:, h * 512:(h + 1) * 512], ones1f[:],
                          nmr[0:1, h * 512:(h + 1) * 512], start=True, stop=True)
        bd.sec("tensor", f)
        t4 = bd.t("tensor")

        def f(dv):
            dv.wait(bd.tick["tensor"], t4)
            dv.tensor_copy(out=R1[:], in_=pbc[:, 0:TPC])
            dv.tensor_copy(out=R2[:], in_=pp0[:, 0:TPC])
            for ch in range(NCH):
                dv.sync()
                dv.tensor_mul(out=rt1[:], in0=xs[:, ch, :], in1=R1[:])
                dv.sync()
                dv.tensor_add(out=xb[:, ch, :], in0=rt1[:], in1=R2[:])
        bd.sec("vector", f)
        tXN = bd.t("vector")

        def f(ac):
            ac.wait(sLT, lT)
            for ch in range(NCH):
                ac.copy(out=wg1b[:, ch, :], in_=wg1f[:, ch, :])
            ac.copy(out=wg2b[:], in_=wg2f[:])
        bd.sec("scalar", f)
        tGW = bd.t("scalar")

        slot_free = [0, 0, 0, 0]
        eW, eC, eE, eM = [], [], [], []
        aMM = []
        aEV = []

        def emit_wl_dma(i):
            src = wa_i if i < 48 else wf_i
            col = (i * 128) if i < 48 else ((i - 48) * 128)
            wait = eC[i - 2] if i >= 2 else None

            def f(sy):
                if wait is not None:
                    sy.wait(bd.tick["scalar"], wait)
                for ch in range(NCH):
                    sy.dma(wl[:, i % 2, ch, :], src[ch, :, col:col + 128], sWp[i % 2], c)
            bd.sec("sync", f)
            eW.append(c.val(sWp[i % 2]))

        def _eM_of_block(i):
            return eM[i] if i < 48 else eM[64 + (i - 48)]

        def emit_wl_cast(i):
            wbf_free = _eM_of_block(i - 2) if i >= 2 else None
            if i < 48:
                def f(ac):
                    ac.wait(sWp[i % 2], eW[i])
                    if wbf_free is not None:
                        ac.wait(bd.tick["tensor"], wbf_free)
                    for ch in range(NCH):
                        ac.copy(out=wbf[:, i % 2, ch, :], in_=wl[:, i % 2, ch, :])
                bd.sec("scalar", f)
            else:
                fi = i - 48

                def f(ac):
                    ac.wait(sWp[i % 2], eW[i])
                    if wbf_free is not None:
                        ac.wait(bd.tick["tensor"], wbf_free)
                    if fi > 0:
                        ac.wait(bd.tick["tensor"], aMM[fi - 1])
                    for ch in range(NCH):
                        ac.activation(out=awb[:, ch, :], in_=wl[:, i % 2, ch, :],
                                      func=AF.Abs)
                    for ch in range(NCH):
                        ac.sign(out=wbf[:, i % 2, ch, :], in_=wl[:, i % 2, ch, :])
                bd.sec("scalar", f)
            eC.append(bd.t("scalar"))

        def emit_proj_mms(i, j, first_wait=None):
            def f(te):
                if first_wait is not None:
                    te.wait(bd.tick["vector"], first_wait)
                te.wait(bd.tick["scalar"], eC[i])
                if j >= 2:
                    te.wait(bd.tick["vector"], eE[j - 2])
                for ch in range(NCH):
                    for h in range(2):
                        te.matmul(pp[j % 2][:, h * 512:(h + 1) * 512],
                                  wbf[:, i % 2, ch, :],
                                  xb[:, ch, h * 512:(h + 1) * 512],
                                  start=(ch == 0), stop=(ch == NCH - 1))
            bd.sec("tensor", f)
            eM.append(bd.t("tensor"))

        def rope_epilogue(cpair, jA, jB, ctab, stab, outA, outB):
            slA, slB = (2 * cpair) % 4, (2 * cpair + 1) % 4

            def f(dv):
                dv.wait(bd.tick["tensor"], eM[jB])
                dv.wait(sT[slA], slot_free[slA])
                dv.wait(sT[slB], slot_free[slB])
                pA, pB = pp[jA % 2], pp[jB % 2]
                dv.tensor_mul(out=rt1[:], in0=pA[:], in1=ctab[:])
                dv.tensor_mul(out=rt2[:], in0=pB[:], in1=stab[:])
                dv.sync()
                dv.tensor_sub(out=est[:, slA, :], in0=rt1[:], in1=rt2[:])
                dv.sync()
                dv.tensor_mul(out=rt1[:], in0=pA[:], in1=stab[:])
                dv.tensor_mul(out=rt2[:], in0=pB[:], in1=ctab[:])
                dv.sync()
                dv.tensor_add(out=est[:, slB, :], in0=rt1[:], in1=rt2[:])
            bd.sec("vector", f)
            eE.append(bd.t("vector"))
            eE.append(bd.t("vector"))
            wS = bd.t("vector")

            def f(gp):
                gp.wait(bd.tick["vector"], wS)
                gp.dma(outA, est[:, slA, :], sT[slA], c)
                gp.dma(outB, est[:, slB, :], sT[slB], c)
            bd.sec("gpsimd", f)
            slot_free[slA] = c.val(sT[slA])
            slot_free[slB] = c.val(sT[slB])

        def plain_epilogue(j, out, slot):
            def f(dv):
                dv.wait(bd.tick["tensor"], eM[j])
                dv.wait(sT[slot], slot_free[slot])
                dv.tensor_copy(out=est[:, slot, :], in_=pp[j % 2][:])
            bd.sec("vector", f)
            eE.append(bd.t("vector"))
            wS = bd.t("vector")

            def f(gp):
                gp.wait(bd.tick["vector"], wS)
                gp.dma(out, est[:, slot, :], sT[slot], c)
            bd.sec("gpsimd", f)
            slot_free[slot] = c.val(sT[slot])

        def ffn_epilogue(j, fi):
            def f(dv):
                dv.wait(bd.tick["tensor"], aMM[fi])
                dv.tensor_scalar(out=alp[:, fi:fi + 1], in0=pbc[:, 0:1],
                                 scalar1=1.0 / 2048.0, scalar2=None, op0=ALU.mult)
                dv.wait(bd.tick["tensor"], eM[j])
                dv.sync()
                dv.tensor_scalar(out=rt1[:], in0=pp[j % 2][:],
                                 scalar1=alp[:, fi:fi + 1],
                                 scalar2=biasc[:, fi:fi + 1],
                                 op0=ALU.mult, op1=ALU.add)
                dv.sync()
                dv.tensor_add(out=xs[:, fi, :], in0=xs[:, fi, :], in1=rt1[:])
            bd.sec("vector", f)
            aEV.append(bd.t("vector"))
            eE.append(bd.t("vector"))
            wS = bd.t("vector")

            def f(gp):
                gp.wait(bd.tick["vector"], wS)
                gp.dma(pxa_o[fi], xs[:, fi, :], sTP, c)
            bd.sec("gpsimd", f)

        j = 0
        for i in range(48):
            emit_wl_dma(i)
            emit_wl_cast(i)
            emit_proj_mms(i, j, first_wait=(tXN if i == 0 else None))
            if i < 16:
                if i % 2 == 1:
                    cp = i // 2
                    rope_epilogue(cp, j - 1, j, cq, sq_, qa_o[cp], qb_o[cp])
            elif i < 32:
                if i % 2 == 1:
                    cp = (i - 16) // 2
                    rope_epilogue(cp + 8, j - 1, j, ck, sk, ka_o[cp], kb_o[cp])
            else:
                plain_epilogue(j, v_o[i - 32], i % 4)
            j += 1

        def f(te):
            te.wait(bd.tick["scalar"], tGW)
            for ch in range(NCH):
                for h in range(2):
                    te.matmul(pbc[0:64, h * 512:(h + 1) * 512],
                              wg1b[:, ch, :], xb[:, ch, h * 512:(h + 1) * 512],
                              start=(ch == 0), stop=(ch == NCH - 1))
        bd.sec("tensor", f)
        tgM = bd.t("tensor")

        def f(dv):
            dv.wait(bd.tick["tensor"], tgM)
            dv.tensor_copy(out=tg[:], in_=pbc[0:64, 0:TPC])
        bd.sec("vector", f)
        tTG = bd.t("vector")

        for gi in range(16):
            def f(te, gi=gi, j=j):
                if gi == 0:
                    te.wait(bd.tick["vector"], tTG)
                if j >= 2:
                    te.wait(bd.tick["vector"], eE[j - 2])
                for h in range(2):
                    te.matmul(pp[j % 2][:, h * 512:(h + 1) * 512],
                              wg2b[:, gi * 128:(gi + 1) * 128],
                              tg[:, h * 512:(h + 1) * 512], start=True, stop=True)
            bd.sec("tensor", f)
            eM.append(bd.t("tensor"))
            plain_epilogue(j, g_o[gi], gi % 4)
            j += 1

        for fi in range(16):
            i = 48 + fi
            emit_wl_dma(i)
            emit_wl_cast(i)

            def f(te, fi=fi):
                te.wait(bd.tick["scalar"], eC[48 + fi])
                if fi == 0:
                    te.wait(bd.tick["vector"], tTG)
                else:
                    te.wait(bd.tick["vector"], aEV[fi - 1])
                for ch in range(NCH):
                    te.matmul(pbc[:, 0:1], awb[:, ch, :], onesb[:],
                              start=(ch == 0), stop=(ch == NCH - 1))
            bd.sec("tensor", f)
            aMM.append(bd.t("tensor"))

            emit_proj_mms(i, j)
            ffn_epilogue(j, fi)
            j += 1

        def f(gp):
            for i in range(4):
                gp.wait(sT[i], c.val(sT[i]))
            gp.wait(sTP, c.val(sTP))
        bd.sec("gpsimd", f)
    return nc


def prep_launch1(inputs):
    x = np.asarray(inputs["L0_attn_hidden_states"], np.float32)
    Wq = np.asarray(inputs["L0_attn_Wq"], np.float32)
    Wk = np.asarray(inputs["L0_attn_Wk"], np.float32)
    Wv = np.asarray(inputs["L0_attn_Wv"], np.float32)
    Wg1 = np.asarray(inputs["L0_attn_Wg1"], np.float32)
    Wg2 = np.asarray(inputs["L0_attn_Wg2"], np.float32)
    W0 = np.asarray(inputs["L0_ffn_weight"], np.float32)
    b0 = np.asarray(inputs["L0_ffn_bias"], np.float32)

    perm = _perm_qfeat()
    wa = np.concatenate([Wq.T[:, perm], Wk.T[:, perm], Wv.T], axis=1)
    wa = np.ascontiguousarray(wa.reshape(NCH, 128, 6144))
    wf = np.ascontiguousarray(W0.T.reshape(NCH, 128, 2048))
    wg1 = np.ascontiguousarray(Wg1.T.reshape(NCH, 128, 64))
    wg2 = np.ascontiguousarray(Wg2.T)
    biasc = np.ascontiguousarray(b0.reshape(16, 128).T)

    in_maps = []
    for core in range(NCORE):
        b, par = core // 2, core % 2
        cos, sin = _rope_tables(par)
        idx = _positions(par)
        xc = np.ascontiguousarray(x[b][idx].T.reshape(NCH, 128, TPC))
        in_maps.append({
            "x": xc, "wa": wa, "wf": wf, "wg1": wg1, "wg2": wg2,
            "bias": biasc,
            "cosq": np.ascontiguousarray(cos * 0.125),
            "sinq": np.ascontiguousarray(sin * 0.125),
            "cosk": cos, "sink": sin,
        })
    return in_maps




def build_launch2():
    nc = bass.Bass("TRN2", target_bir_lowering=False)
    dt = nc.dram_tensor
    qa_i = dt("qa", [8, 128, TPC], BF16, kind="ExternalInput").ap()
    qb_i = dt("qb", [8, 128, TPC], BF16, kind="ExternalInput").ap()
    ka_i = dt("ka", [8, 128, S], BF16, kind="ExternalInput").ap()
    kb_i = dt("kb", [8, 128, S], BF16, kind="ExternalInput").ap()
    vt_i = dt("vt", [8, 16, 128, 256], BF16, kind="ExternalInput").ap()
    g_i = dt("g", [NCH, 128, TPC], BF16, kind="ExternalInput").ap()
    caus_i = dt("caus", [128, 256], F32, kind="ExternalInput").ap()
    m2_i = dt("m2", [128, 64], F32, kind="ExternalInput").ap()
    idb_i = dt("idb", [128, 128], BF16, kind="ExternalInput").ap()

    o_o = dt("o", [NCH, 128, TPC], BF16, kind="ExternalOutput").ap()

    with ExitStack() as ctx:
        sb = lambda n, s, d: ctx.enter_context(nc.sbuf_tensor(n, s, d))
        qa_s = sb("qa_s", [128, 8, TPC], BF16)
        qb_s = sb("qb_s", [128, 8, TPC], BF16)
        qa2 = sb("qa2", [128, 8, TPC], BF16)
        qb2 = sb("qb2", [128, 8, TPC], BF16)
        g_s = sb("g_s", [128, NCH, TPC], BF16)
        ka_s = sb("ka_s", [128, 2, S], BF16)
        kb_s = sb("kb_s", [128, 2, S], BF16)
        vt_s = sb("vt_s", [128, 2, 16, 256], BF16)
        p_s = sb("p_s", [128, 2048], BF16)
        pts = sb("pts", [128, 2, 128], BF16)
        oh = sb("oh", [64, TPC], F32)
        obf = sb("obf", [64, 2, TPC], BF16)
        sgb = sb("sgb", [64, TPC], F32)
        kmaf = sb("kmaf", [128, 8], F32)
        kmbf = sb("kmbf", [128, 8], F32)
        kmab = sb("kmab", [128, 8], BF16)
        kmbb = sb("kmbb", [128, 8], BF16)
        kmxs = sb("kmxs", [1, 1], F32)
        kmb = sb("kmb", [128, 1], F32)
        gatef = sb("gatef", [128, 64], F32)
        tmpc = sb("tmpc", [128, 64], F32)
        cnt = sb("cnt", [128, 64], F32)
        biasal = sb("biasal", [128, 64], F32)
        mhat = sb("mhat", [128, 8], F32)
        dacc = sb("dacc", [128, 8], F32)
        den = sb("den", [128, 1], F32)
        rec = sb("rec", [128, 1], F32)
        rrow = sb("rrow", [1, TPC], F32)
        caus = sb("caus_s", [128, 256], F32)
        m2c = sb("m2c_s", [128, 64], F32)
        identb = sb("identb", [128, 128], BF16)
        onesb = sb("onesb", [128, 1], BF16)
        ones1f = sb("ones1f", [1, 128], F32)
        eps1 = sb("eps1", [1, 1], F32)
        pscr = ctx.enter_context(nc.psum_tensor("pscr", [128, 2048], F32))
        ppt0 = ctx.enter_context(nc.psum_tensor("ppt0", [128, 128], BF16))
        ppt1 = ctx.enter_context(nc.psum_tensor("ppt1", [128, 128], BF16))
        pm2 = ctx.enter_context(nc.psum_tensor("pm2", [128, 512], F32))
        ppt = [ppt0, ppt1]

        bd = Bld(nc, ctx)
        c = bd.c
        tV, tA, tP, tG = (bd.tick[e] for e in ("vector", "scalar", "tensor", "gpsimd"))
        sLQ = bd.sem(ctx, "sLQ")
        sK = [bd.sem(ctx, "sK0"), bd.sem(ctx, "sK1")]
        sO = [bd.sem(ctx, "sO0"), bd.sem(ctx, "sO1")]

        def f(sy):
            for t in range(8):
                sy.dma(qa_s[:, t, :], qa_i[t], sLQ, c)
                sy.dma(qb_s[:, t, :], qb_i[t], sLQ, c)
            for ch in range(NCH):
                sy.dma(g_s[:, ch, :], g_i[ch], sLQ, c)
            sy.dma(caus[:], caus_i, sLQ, c)
            sy.dma(m2c[:], m2_i, sLQ, c)
            sy.dma(identb[:], idb_i, sLQ, c)
        bd.sec("sync", f)
        lQ = c.val(sLQ)

        def f(gp):
            gp.memset(onesb[:], 1.0)
            gp.memset(ones1f[:], 1.0)
            gp.memset(eps1[:], 1e-6)
        bd.sec("gpsimd", f)
        tOn = bd.t("gpsimd")

        def f(ac):
            ac.wait(sLQ, lQ)
            for t in range(8):
                ac.activation(out=qa2[:, t, :], in_=qa_s[:, t, :], func=AF.Square)
                ac.activation(out=qb2[:, t, :], in_=qb_s[:, t, :], func=AF.Square)
        bd.sec("scalar", f)

        kload = []
        kuse = []

        for grp in range(8):
            buf = grp % 2

            def f(sy, grp=grp, buf=buf):
                if grp >= 2:
                    sy.wait(tP, kuse[grp - 2])
                sy.dma(ka_s[:, buf, :], ka_i[grp], sK[buf], c)
                sy.dma(kb_s[:, buf, :], kb_i[grp], sK[buf], c)
                for tc in range(16):
                    sy.dma(vt_s[:, buf, tc, :], vt_i[grp, tc], sK[buf], c)
            bd.sec("sync", f)
            kload.append(c.val(sK[buf]))

            def f(dv, grp=grp, buf=buf):
                dv.wait(sK[buf], kload[grp])
                dv.wait(tP, c.val(tP))
                dv.tensor_reduce(out=kmaf[:],
                                 in_=ka_s[:, buf, :].rearrange("p (n x) -> p n x", x=256),
                                 axis=AX.X, op=ALU.add)
                dv.tensor_reduce(out=kmbf[:],
                                 in_=kb_s[:, buf, :].rearrange("p (n x) -> p n x", x=256),
                                 axis=AX.X, op=ALU.add)
                dv.sync()
                dv.tensor_copy(out=kmab[:], in_=kmaf[:])
                dv.tensor_copy(out=kmbb[:], in_=kmbf[:])
            bd.sec("vector", f)
            tKM = bd.t("vector")

            def f(ac, grp=grp, buf=buf):
                ac.wait(sK[buf], kload[grp])
                ac.wait(tP, c.val(tP))
                ac.activation(out=p_s[:, 0:2048], in_=ka_s[:, buf, :], func=AF.Square)
            bd.sec("scalar", f)
            tKA2 = bd.t("scalar")

            def f(te, grp=grp):
                te.wait(tA, tKA2)
                if grp == 0:
                    te.wait(tG, tOn)
                for sg in range(4):
                    te.matmul(pscr[0:1, sg * 512:(sg + 1) * 512], onesb[:],
                              p_s[:, sg * 512:(sg + 1) * 512], start=True, stop=False)
            bd.sec("tensor", f)
            tkn1 = bd.t("tensor")

            def f(ac, buf=buf):
                ac.wait(tP, tkn1)
                ac.activation(out=p_s[:, 0:2048], in_=kb_s[:, buf, :], func=AF.Square)
            bd.sec("scalar", f)
            tKB2 = bd.t("scalar")

            def f(te):
                te.wait(tA, tKB2)
                for sg in range(4):
                    te.matmul(pscr[0:1, sg * 512:(sg + 1) * 512], onesb[:],
                              p_s[:, sg * 512:(sg + 1) * 512], start=False, stop=True)
            bd.sec("tensor", f)
            tkn2 = bd.t("tensor")

            def f(dv):
                dv.wait(tP, tkn2)
                dv.reduce_max(out=kmxs[0:1, 0:1], in_=pscr[0:1, 0:2048], axis=AX.X)
            bd.sec("vector", f)
            tKMX = bd.t("vector")

            def f(ac):
                ac.wait(tV, tKMX)
                ac.activation(out=kmxs[:], in_=kmxs[:], func=AF.Sqrt)
            bd.sec("scalar", f)
            tSQK = bd.t("scalar")

            def f(te):
                te.wait(tA, tSQK)
                te.matmul(pm2[:, 0:1], ones1f[:], kmxs[:], start=True, stop=True)
            bd.sec("tensor", f)
            tKB = bd.t("tensor")

            def f(dv):
                dv.wait(tP, tKB)
                dv.tensor_copy(out=kmb[:], in_=pm2[:, 0:1])
            bd.sec("vector", f)
            tKMB = bd.t("vector")

            for hh in range(4):
                h = 4 * grp + hh
                hr = 32 * hh
                tp = (hr, 0)
                gch, gro = h // 2, 64 * (h % 2)

                def f(te, grp=grp, hr=hr, tp=tp, hh=hh, wV=None, wA=None):
                    te.wait(tV, tKMB)
                    te.wait(tV, c.val(tV))
                    te.wait(tA, c.val(tA))
                    for ti in range(8):
                        q_t = qa_s[hr:hr + 32, grp, ti * 128:(ti + 1) * 128]
                        te.matmul(pm2[:, 8 + ti * 8:16 + ti * 8], q_t,
                                  kmab[hr:hr + 32, :], start=True, stop=False,
                                  tile_position=tp)
                        q_tb = qb_s[hr:hr + 32, grp, ti * 128:(ti + 1) * 128]
                        te.matmul(pm2[:, 8 + ti * 8:16 + ti * 8], q_tb,
                                  kmbb[hr:hr + 32, :], start=False, stop=True,
                                  tile_position=tp)
                        te.matmul(pm2[:, 72 + ti:73 + ti],
                                  qa2[hr:hr + 32, grp, ti * 128:(ti + 1) * 128],
                                  onesb[hr:hr + 32, :], start=True, stop=False,
                                  tile_position=tp)
                        te.matmul(pm2[:, 72 + ti:73 + ti],
                                  qb2[hr:hr + 32, grp, ti * 128:(ti + 1) * 128],
                                  onesb[hr:hr + 32, :], start=False, stop=True,
                                  tile_position=tp)
                bd.sec("tensor", f)
                tGate = bd.t("tensor")

                def f(ac):
                    ac.wait(tP, tGate)
                    ac.activation(out=mhat[:], in_=pm2[:, 72:80], func=AF.Sqrt)
                bd.sec("scalar", f)
                tMH = bd.t("scalar")

                def f(dv):
                    dv.wait(tP, tGate)
                    dv.wait(tA, c.val(tA))
                    dv.tensor_add(out=gatef[:], in0=pm2[:, 8:72], in1=m2c[:])
                    dv.sync()
                    gv = gatef[:].rearrange("p (t n) -> p t n", n=8)
                    cv = cnt[:].rearrange("p (t n) -> p t n", n=8)
                    for n in range(8):
                        dv.tensor_tensor(out=tmpc[:].rearrange("p (t n) -> p t n", n=8),
                                         in0=gv,
                                         in1=gv[:, :, n:n + 1].broadcast_to([128, 8, 8]),
                                         op=ALU.is_gt)
                        dv.sync()
                        dv.tensor_reduce(out=cv[:, :, n:n + 1], in_=tmpc[:].rearrange("p (t n) -> p t n", n=8),
                                         axis=AX.X, op=ALU.add)
                        dv.sync()
                    dv.tensor_scalar(out=biasal[:], in0=cnt[:], scalar1=3.5,
                                     scalar2=NEG, op0=ALU.is_gt, op1=ALU.mult)
                    dv.wait(tA, tMH)
                    dv.sync()
                    dv.tensor_mul(out=mhat[:], in0=mhat[:],
                                  in1=kmb[:, 0:1].broadcast_to([128, 8]))
                    dv.sync()
                    dv.tensor_tensor(out=biasal[:].rearrange("p (t n) -> p t n", n=8),
                                     in0=biasal[:].rearrange("p (t n) -> p t n", n=8),
                                     in1=mhat[:].rearrange("p t -> p t ()").broadcast_to([128, 8, 8]),
                                     op=ALU.subtract)
                bd.sec("vector", f)
                tTK = bd.t("vector")

                def f(ac, gch=gch, gro=gro):
                    ac.wait(tV, c.val(tV))
                    ac.activation(out=sgb[:, 0:TPC], in_=g_s[gro:gro + 64, gch, :],
                                  func=AF.Sigmoid)
                bd.sec("scalar", f)
                tSG = bd.t("scalar")

                lastPV = None
                for ti in range(8):
                    nk = (ti + 1) * 256
                    nseg = (nk + 511) // 512

                    def f(te, ti=ti, nseg=nseg, nk=nk, grp=grp, hr=hr, tp=tp, buf=buf):
                        te.wait(tV, tTK)
                        te.wait(tA, c.val(tA))
                        for sg in range(nseg):
                            s0 = sg * 512
                            s1 = min(nk, s0 + 512)
                            te.matmul(pscr[:, s0:s1],
                                      qa_s[hr:hr + 32, grp, ti * 128:(ti + 1) * 128],
                                      ka_s[hr:hr + 32, buf, s0:s1],
                                      start=True, stop=False, tile_position=tp)
                            te.matmul(pscr[:, s0:s1],
                                      qb_s[hr:hr + 32, grp, ti * 128:(ti + 1) * 128],
                                      kb_s[hr:hr + 32, buf, s0:s1],
                                      start=False, stop=True, tile_position=tp)
                    bd.sec("tensor", f)
                    tSC = bd.t("tensor")

                    def f(dv, ti=ti):
                        dv.wait(tP, tSC)
                        dv.tensor_add(out=pscr[:, ti * 256:(ti + 1) * 256],
                                      in0=pscr[:, ti * 256:(ti + 1) * 256], in1=caus[:])
                    bd.sec("vector", f)
                    tCA = bd.t("vector")

                    def f(ac, ti=ti):
                        ac.wait(tV, tCA)
                        ac.wait(tP, c.val(tP))
                        for n in range(ti + 1):
                            ac.activation(out=p_s[:, n * 256:(n + 1) * 256],
                                          in_=pscr[:, n * 256:(n + 1) * 256],
                                          func=AF.Exp,
                                          bias=biasal[:, ti * 8 + n:ti * 8 + n + 1],
                                          scale=1.0,
                                          accum_out=dacc[:, n:n + 1])
                    bd.sec("scalar", f)
                    tEX = bd.t("scalar")

                    def f(dv, ti=ti):
                        dv.wait(tA, tEX)
                        dv.wait(tP, c.val(tP))
                        dv.tensor_reduce(out=den[:], in_=dacc[:, 0:ti + 1],
                                         axis=AX.X, op=ALU.add)
                        dv.sync()
                        dv.reciprocal(out=rec[:], in_=den[:])
                        dv.sync()
                        dv.tensor_scalar(out=p_s[:, 0:(ti + 1) * 256],
                                         in0=p_s[:, 0:(ti + 1) * 256],
                                         scalar1=rec[:, 0:1], scalar2=None,
                                         op0=ALU.mult)
                    bd.sec("vector", f)
                    tDD = bd.t("vector")

                    for kk in range(2 * (ti + 1)):
                        def f(te, kk=kk, ti=ti):
                            if kk == 0:
                                te.wait(tV, tDD)
                            if kk >= 2:
                                te.wait(tV, c.val(tV))
                            te.transpose(ppt[kk % 2][:], p_s[:, kk * 128:(kk + 1) * 128],
                                         identb[:])
                        bd.sec("tensor", f)
                        tTR = bd.t("tensor")

                        def f(dv, kk=kk):
                            dv.wait(tP, tTR)
                            dv.tensor_copy(out=pts[:, kk % 2, :], in_=ppt[kk % 2][:])
                        bd.sec("vector", f)
                        tEV = bd.t("vector")

                        def f(te, kk=kk, ti=ti, buf=buf, hh=hh):
                            te.wait(tV, tEV)
                            te.matmul(pm2[0:64, 0:128],
                                      vt_s[:, buf, kk, hh * 64:(hh + 1) * 64],
                                      pts[:, kk % 2, :],
                                      start=(kk == 0), stop=(kk == 2 * ti + 1))
                        bd.sec("tensor", f)
                    lastPV = bd.t("tensor")

                    def f(dv, ti=ti, lastPV=lastPV):
                        dv.wait(tP, lastPV)
                        dv.tensor_copy(out=oh[:, ti * 128:(ti + 1) * 128],
                                       in_=pm2[0:64, 0:128])
                    bd.sec("vector", f)
                tOH = bd.t("vector")

                def f(ac, tOH=tOH):
                    ac.wait(tV, tOH)
                    ac.activation(out=p_s[0:64, 0:TPC], in_=oh[:, 0:TPC], func=AF.Square)
                bd.sec("scalar", f)
                tOS = bd.t("scalar")

                def f(te, tOS=tOS):
                    te.wait(tA, tOS)
                    for hf in range(2):
                        te.matmul(pscr[0:1, hf * 512:(hf + 1) * 512], onesb[0:64, :],
                                  p_s[0:64, hf * 512:(hf + 1) * 512],
                                  start=True, stop=True)
                bd.sec("tensor", f)
                tRS = bd.t("tensor")

                def f(ac, tRS=tRS):
                    ac.wait(tP, tRS)
                    ac.activation(out=rrow[:, 0:TPC], in_=pscr[0:1, 0:TPC],
                                  func=AF.Sqrt, bias=eps1[:, 0:1], scale=1.0 / 64.0)
                bd.sec("scalar", f)
                tRQ = bd.t("scalar")

                def f(dv, tRQ=tRQ):
                    dv.wait(tA, tRQ)
                    dv.reciprocal(out=rrow[:], in_=rrow[:])
                bd.sec("vector", f)
                tRR = bd.t("vector")

                slot = h % 2
                for hf in range(2):
                    def f(te, hf=hf, tRR=tRR):
                        te.wait(tV, tRR if hf == 0 else c.val(tV))
                        te.matmul(pm2[0:64, 0:512],
                                  ones1f[0:1, 0:64],
                                  rrow[0:1, hf * 512:(hf + 1) * 512],
                                  start=True, stop=True)
                    bd.sec("tensor", f)
                    tR2 = bd.t("tensor")

                    def f(dv, hf=hf, tR2=tR2, slot=slot, h=h):
                        dv.wait(tP, tR2)
                        dv.wait(tA, tSG)
                        if h >= 2:
                            dv.wait(sO[slot], c.val(sO[slot]))
                        dv.tensor_mul(out=oh[:, hf * 512:(hf + 1) * 512],
                                      in0=oh[:, hf * 512:(hf + 1) * 512],
                                      in1=sgb[:, hf * 512:(hf + 1) * 512])
                        dv.sync()
                        dv.tensor_mul(out=obf[:, slot, hf * 512:(hf + 1) * 512],
                                      in0=oh[:, hf * 512:(hf + 1) * 512],
                                      in1=pm2[0:64, 0:512])
                    bd.sec("vector", f)
                tOB = bd.t("vector")

                def f(gp, slot=slot, gch=gch, gro=gro, tOB=tOB):
                    gp.wait(tV, tOB)
                    gp.dma(o_o[gch, gro:gro + 64, :], obf[:, slot, :], sO[slot], c)
                bd.sec("gpsimd", f)

            kuse.append(bd.t("tensor"))

        def f(gp):
            gp.wait(sO[0], c.val(sO[0]))
            gp.wait(sO[1], c.val(sO[1]))
        bd.sec("gpsimd", f)
    return nc


def prep_launch2(l1res):
    def bf(a):
        a = np.asarray(a)
        if a.dtype.kind == 'V':
            a = a.view(BF)
        return a.astype(BF) if a.dtype != BF else a

    caus, m2, ident = {}, _topk_m2(), np.eye(128, dtype=BF)
    for par in (0, 1):
        s_ = np.arange(128)[:, None]
        t_ = np.arange(256)[None, :]
        caus[par] = np.where(t_ <= s_ + 128 * par, 0.0, NEG).astype(np.float32)

    in_maps = []
    for core in range(NCORE):
        b, par = core // 2, core % 2
        r0, r1 = l1res[2 * b], l1res[2 * b + 1]
        ka = np.zeros((8, 128, S), BF)
        kb = np.zeros((8, 128, S), BF)
        ka[:, :, _positions(0)] = bf(r0["ka"])
        ka[:, :, _positions(1)] = bf(r1["ka"])
        kb[:, :, _positions(0)] = bf(r0["kb"])
        kb[:, :, _positions(1)] = bf(r1["kb"])
        vfull = np.zeros((2048, S), BF)
        vfull[:, _positions(0)] = bf(r0["v"]).reshape(2048, TPC)
        vfull[:, _positions(1)] = bf(r1["v"]).reshape(2048, TPC)
        vt = np.stack([np.ascontiguousarray(
            vfull[256 * g4:256 * (g4 + 1), :].T).reshape(16, 128, 256)
            for g4 in range(8)])
        me = l1res[core]
        in_maps.append({
            "qa": bf(me["qa"]), "qb": bf(me["qb"]), "ka": ka, "kb": kb,
            "vt": vt, "g": bf(me["g"]), "caus": caus[par], "m2": m2,
            "idb": ident,
        })
    return in_maps


def _topk_m2():
    m = np.zeros((8, 8), np.float32)
    for t in range(8):
        m[t, t] = 1.0e30
        m[t, t + 1:] = NEG
    return np.tile(m.reshape(1, 64), (128, 1)).astype(np.float32)


TCH = 8 * 131


def build_launch3():
    nc = bass.Bass("TRN2", target_bir_lowering=False)
    dt = nc.dram_tensor
    o_i = dt("o", [NCH, 128, TCH], BF16, kind="ExternalInput").ap()
    pxa_i = dt("pxa", [NCH, 128, TCH], F32, kind="ExternalInput").ap()
    onw_i = dt("onw", [128, 1], F32, kind="ExternalInput").ap()
    wo_i = dt("wo", [NCH, 128, 2048], F32, kind="ExternalInput").ap()
    wqkv_i = dt("wqkv", [NCH, 128, 6144], F32, kind="ExternalInput").ap()
    wgt_i = dt("wgt", [NCH, 128, 2048], F32, kind="ExternalInput").ap()
    wab_i = dt("wab", [NCH, 128, 64], F32, kind="ExternalInput").ap()
    cw_i = dt("cw", [128, 48, 4], F32, kind="ExternalInput").ap()
    alog_i = dt("alog", [16, 1], F32, kind="ExternalInput").ap()
    dtb_i = dt("dtb", [16, 1], F32, kind="ExternalInput").ap()
    w1_i = dt("w1", [NCH, 128, 8192], F32, kind="ExternalInput").ap()
    w3_i = dt("w3", [NCH, 128, 8192], F32, kind="ExternalInput").ap()
    w2r_i = dt("w2r", [128, 131072], F32, kind="ExternalInput").ap()
    h128_i = dt("h128", [128, 128], BF16, kind="ExternalInput").ap()
    h64_i = dt("h64", [128, 128], BF16, kind="ExternalInput").ap()

    qf_o = dt("qf", [32, 128, TPC], BF16, kind="ExternalOutput").ap()
    v_o = dt("vv", [16, 128, TPC], BF16, kind="ExternalOutput").ap()
    sg_o = dt("sg", [16, 128, TPC], BF16, kind="ExternalOutput").ap()
    beta_o = dt("beta", [16, TPC], F32, kind="ExternalOutput").ap()
    gdec_o = dt("gdec", [16, TPC], F32, kind="ExternalOutput").ap()
    pxb_o = dt("pxb", [NCH, 128, TPC], F32, kind="ExternalOutput").ap()

    w2p = dt("w2p", [16, 128, 64, 128], BF16, kind="Internal").ap()

    NH = TCH
    SEGS = [(0, 512), (512, 1024), (1024, NH)]

    with ExitStack() as ctx:
        sb = lambda n, s, d: ctx.enter_context(nc.sbuf_tensor(n, s, d))
        x1 = sb("x1", [128, NCH, NH], F32)
        xn = sb("xn", [128, NCH, NH], BF16)
        wl = sb("wl", [128, 2, NCH, 128], F32)
        wbf = sb("wbf", [128, 2, NCH, 128], BF16)
        cb = sb("cb", [128, 512], F32)
        sqb = sb("sqb", [128, TPC], BF16)
        cwc = sb("cwc", [128, 48, 4], F32)
        onw = sb("onw_s", [128, 1], F32)
        alog = sb("alog_s", [16, 1], F32)
        dtb = sb("dtb_s", [16, 1], F32)
        ea = sb("ea", [16, 1], F32)
        h128c = sb("h128c", [128, 128], BF16)
        h64c = sb("h64c", [128, 128], BF16)
        onesb = sb("onesb", [128, 1], BF16)
        ones1f = sb("ones1f", [1, 128], F32)
        onesq1 = sb("onesq1", [1, 128], F32)
        one16 = sb("one16", [128, 1], F32)
        epsc = sb("epsc", [1, 1], F32)
        ctxAB = ExitStack()
        sab = lambda n, s, d: ctxAB.enter_context(nc.sbuf_tensor(n, s, d))
        ob = sab("ob", [128, NCH, NH], BF16)
        mu = sab("mu", [1, NH], F32)
        rstd = sab("rstd", [1, NH], F32)
        nmr = sab("nmr", [1, NH], F32)
        R1 = sab("R1", [128, NH], BF16)
        R2 = sab("R2", [128, NH], BF16)
        rt1 = sab("rt1", [128, NH], F32)
        sq2 = sab("sq2", [128, 2, TPC], BF16)
        pp0 = ctx.enter_context(nc.psum_tensor("pp0", [128, 1536], F32))
        pp1 = ctx.enter_context(nc.psum_tensor("pp1", [128, 1536], F32))
        pr = ctx.enter_context(nc.psum_tensor("pr", [128, 1024], F32))
        pp = [pp0, pp1]

        bd = Bld(nc, ctx)
        c = bd.c
        tV, tA, tP, tG = (bd.tick[e] for e in ("vector", "scalar", "tensor", "gpsimd"))
        sLX = bd.sem(ctx, "sLX")
        sWp = [bd.sem(ctx, "sW0"), bd.sem(ctx, "sW1")]
        sW2 = [bd.sem(ctx, "sW2a"), bd.sem(ctx, "sW2b")]
        sWD = [bd.sem(ctx, "sWDa")]
        sT = [bd.sem(ctx, "sTa"), bd.sem(ctx, "sTb")]
        sTP = bd.sem(ctx, "sTP")
        sPW = bd.sem(ctx, "sPW")

        def f(sy):
            for ch in range(NCH):
                sy.dma(ob[:, ch, :], o_i[ch], sLX, c)
                sy.dma(x1[:, ch, :], pxa_i[ch], sLX, c)
            sy.dma(onw[:], onw_i, sLX, c)
            sy.dma(cwc[:], cw_i, sLX, c)
            sy.dma(alog[:], alog_i, sLX, c)
            sy.dma(dtb[:], dtb_i, sLX, c)
            sy.dma(h128c[:], h128_i, sLX, c)
            sy.dma(h64c[:], h64_i, sLX, c)
        bd.sec("sync", f)
        lX = c.val(sLX)

        def f(gp):
            gp.memset(onesb[:], 1.0)
            gp.memset(ones1f[:], 1.0)
            gp.memset(onesq1[:], float(128 ** -0.5))
            gp.memset(one16[:], 1.0)
            gp.memset(epsc[:], 1e-6)
        bd.sec("gpsimd", f)
        tOn = bd.t("gpsimd")

        WBLK = ([(wo_i, i * 128, 128) for i in range(16)] +
                [(wqkv_i, i * 128, 128) for i in range(48)] +
                [(wgt_i, i * 128, 128) for i in range(16)] +
                [(wab_i, 0, 64)] +
                [blk for i in range(64)
                 for blk in ((w1_i, i * 128, 128), (w3_i, i * 128, 128))] +
                [blk for i in range(64)
                 for blk in ((w1_i, i * 128, 128), (w3_i, i * 128, 128))])

        eW, eC, eM = [], [], []
        eE = []

        def emit_wdma(i):
            src, col, ww = WBLK[i]
            wait = eC[i - 2] if i >= 2 else None

            def f(sy, i=i, src=src, col=col, ww=ww, wait=wait):
                if wait is not None:
                    sy.wait(tA, wait)
                for ch in range(NCH):
                    sy.dma(wl[:, i % 2, ch, 0:ww], src[ch, :, col:col + ww],
                           sWp[i % 2], c)
            bd.sec("sync", f)
            eW.append(c.val(sWp[i % 2]))

        def emit_wcast(i, use_onw=False):
            ww = WBLK[i][2]
            wbf_free = eM[i - 2] if i >= 2 else None

            def f(ac, i=i, ww=ww, wbf_free=wbf_free, use_onw=use_onw):
                ac.wait(sWp[i % 2], eW[i])
                if i == 0:
                    ac.wait(sLX, lX)
                if wbf_free is not None:
                    ac.wait(tP, wbf_free)
                for ch in range(NCH):
                    if use_onw:
                        ac.activation(out=wbf[:, i % 2, ch, 0:ww],
                                      in_=wl[:, i % 2, ch, 0:ww],
                                      func=AF.Copy, bias=0.0, scale=onw[:, 0:1])
                    else:
                        ac.copy(out=wbf[:, i % 2, ch, 0:ww], in_=wl[:, i % 2, ch, 0:ww])
            bd.sec("scalar", f)
            eC.append(bd.t("scalar"))

        def emit_pmms(i, j, rhs=None, first_wait=None, segs=None):
            ww = WBLK[i][2]
            segs = segs or SEGS
            wE = eE[j - 2] if j >= 2 else None

            def f(te, i=i, j=j, ww=ww, rhs=rhs, fw=first_wait, segs=segs, wE=wE):
                if fw is not None:
                    te.wait(*fw)
                te.wait(tA, eC[i])
                if wE is not None:
                    for tk, val in wE:
                        te.wait(bd.tick[tk], val)
                dst = pp[j % 2]
                for ch in range(NCH):
                    rr = xn[:, ch, :] if rhs is None else rhs(ch)
                    for (s0, s1) in segs:
                        te.matmul(dst[0:ww, s0:s1], wbf[:, i % 2, ch, 0:ww],
                                  rr[:, s0:s1], start=(ch == 0), stop=(ch == NCH - 1))
            bd.sec("tensor", f)
            eM.append(bd.t("tensor"))

        for i in range(16):
            emit_wdma(i)
            emit_wcast(i, use_onw=True)
            emit_pmms(i, i, rhs=lambda ch: ob[:, ch, :],
                      first_wait=((tG, tOn) if i == 0 else None))

            def f(dv, i=i):
                dv.wait(tP, eM[i])
                dv.tensor_add(out=x1[:, i, :], in0=x1[:, i, :], in1=pp[i % 2][:, 0:NH])
            bd.sec("vector", f)
            eE.append([("vector", bd.t("vector"))])

        def f(dv):
            dv.wait(tV, eE[15][0][1])
            for ch in range(NCH):
                dv.tensor_copy(out=xn[:, ch, :], in_=x1[:, ch, :])
        bd.sec("vector", f)
        tXB = bd.t("vector")

        peR = []
        for ch in range(NCH):
            def fa(ac, ch=ch):
                if ch == 0:
                    ac.wait(tV, tXB)
                if ch >= 1:
                    ac.wait(tP, peR[ch - 1])
                ac.activation(out=sq2[:, ch % 2, 0:TPC], in_=xn[:, ch, 0:TPC],
                              func=AF.Square)
                ac.activation(out=sqb[:, 0:(NH - TPC)], in_=xn[:, ch, TPC:NH],
                              func=AF.Square)
            bd.sec("scalar", fa)
            sqd = bd.t("scalar")

            def ft(te, ch=ch, sqd=sqd):
                if ch == 0:
                    te.wait(tV, tXB)
                stt = (ch == 0)
                sp = (ch == NCH - 1)
                for (s0, s1) in SEGS:
                    te.matmul(pp0[0:1, s0:s1], onesb[:], xn[:, ch, s0:s1],
                              start=stt, stop=sp)
                te.wait(tA, sqd)
                for (s0, s1) in [(0, 512), (512, 1024)]:
                    te.matmul(pp1[0:1, s0:s1], onesb[:], sq2[:, ch % 2, s0:s1],
                              start=stt, stop=sp)
                te.matmul(pp1[0:1, 1024:NH], onesb[:], sqb[:, 0:(NH - TPC)],
                          start=stt, stop=sp)
            bd.sec("tensor", ft)
            peR.append(bd.t("tensor"))
        tStats = bd.t("tensor")

        def f(dv):
            dv.wait(tP, tStats)
            dv.tensor_scalar(out=mu[:], in0=pp0[0:1, 0:NH], scalar1=1.0 / D,
                             scalar2=None, op0=ALU.mult)
            dv.tensor_scalar(out=rstd[:], in0=pp1[0:1, 0:NH], scalar1=1.0 / D,
                             scalar2=1e-5, op0=ALU.mult, op1=ALU.add)
            dv.sync()
            dv.tensor_mul(out=nmr[:], in0=mu[:], in1=mu[:])
            dv.sync()
            dv.tensor_sub(out=rstd[:], in0=rstd[:], in1=nmr[:])
        bd.sec("vector", f)
        t1 = bd.t("vector")

        def f(ac):
            ac.wait(tV, t1)
            ac.activation(out=rstd[:], in_=rstd[:], func=AF.Sqrt)
        bd.sec("scalar", f)
        t2 = bd.t("scalar")

        def f(dv):
            dv.wait(tA, t2)
            dv.reciprocal(out=rstd[:], in_=rstd[:])
            dv.sync()
            dv.scalar_tensor_tensor(out=nmr[:], in0=mu[:], scalar=-1.0, in1=rstd[:],
                                    op0=ALU.mult, op1=ALU.mult)
        bd.sec("vector", f)
        t3 = bd.t("vector")

        def f(te):
            te.wait(tV, t3)
            for (s0, s1) in SEGS:
                te.matmul(pp0[:, s0:s1], ones1f[:], rstd[0:1, s0:s1],
                          start=True, stop=True)
                te.matmul(pp1[:, s0:s1], ones1f[:], nmr[0:1, s0:s1],
                          start=True, stop=True)
        bd.sec("tensor", f)
        t4 = bd.t("tensor")

        def f(dv):
            dv.wait(tP, t4)
            dv.tensor_copy(out=R1[:], in_=pp0[:, 0:NH])
            dv.tensor_copy(out=R2[:], in_=pp1[:, 0:NH])
            dv.sync()
            for ch in range(NCH):
                dv.tensor_mul(out=rt1[:], in0=x1[:, ch, :], in1=R1[:])
                dv.sync()
                dv.tensor_add(out=xn[:, ch, :], in0=rt1[:], in1=R2[:])
                dv.sync()
        bd.sec("vector", f)
        tXN = bd.t("vector")

        def phase_barrier():
            vals = {e: bd.t(e) for e in Bld.ENGS}
            dsem = [(sLX, c.val(sLX)), (sWp[0], c.val(sWp[0])),
                    (sWp[1], c.val(sWp[1])), (sT[0], c.val(sT[0])),
                    (sT[1], c.val(sT[1])), (sTP, c.val(sTP)),
                    (sW2[0], c.val(sW2[0])), (sW2[1], c.val(sW2[1])),
                    (sWD[0], c.val(sWD[0])), (sPW, c.val(sPW))]
            for e in Bld.ENGS:
                def fb(eng, e=e, vals=vals, dsem=dsem):
                    for o in Bld.ENGS:
                        if o != e:
                            eng.wait(bd.tick[o], vals[o])
                    for sm, vv in dsem:
                        eng.wait(sm, vv)
                bd.sec(e, fb)

        phase_barrier()
        ctxAB.close()
        ctxC = ExitStack()
        scc = lambda n, s, d: ctxC.enter_context(nc.sbuf_tensor(n, s, d))
        acc = scc("acc", [128, TPC], F32)
        est = scc("est", [128, 2, TPC], BF16)
        lrow = scc("lrow", [1, TPC], F32)
        abx = scc("abx", [16, 3, NH], F32)
        bet = scc("bet", [16, NH], F32)
        gde = scc("gde", [16, NH], F32)

        j = 16

        for oc in range(48):
            i = 16 + oc
            emit_wdma(i)
            emit_wcast(i)
            emit_pmms(i, j, first_wait=((tV, tXN) if oc == 0 else None))
            slot = oc % 2

            def f(dv, oc=oc, j=j):
                dv.wait(tP, eM[j])
                dv.sync()
                src = pp[j % 2][:, 0:NH].rearrange("p (t x) -> p t x", x=131)
                av = acc[:].rearrange("p (t x) -> p t x", x=128)
                dv.tensor_scalar(out=av, in0=src[:, :, 0:128],
                                 scalar1=cwc[:, oc, 0:1], scalar2=None, op0=ALU.mult)
                for jj in range(1, 4):
                    dv.sync()
                    dv.scalar_tensor_tensor(out=av, in0=src[:, :, jj:jj + 128],
                                            scalar=cwc[:, oc, jj:jj + 1], in1=av,
                                            op0=ALU.mult, op1=ALU.add)
            bd.sec("vector", f)
            eE.append([("vector", bd.t("vector"))])
            tCV = bd.t("vector")

            if oc < 32:
                def f(ac, tCV=tCV):
                    ac.wait(tV, tCV)
                    ac.activation(out=acc[:], in_=acc[:], func=AF.Silu)
                    ac.sync()
                    ac.activation(out=sqb[:, 0:TPC], in_=acc[:], func=AF.Square)
                bd.sec("scalar", f)
                tSQ = bd.t("scalar")

                def f(te, tSQ=tSQ):
                    te.wait(tA, tSQ)
                    te.wait(tV, c.val(tV))
                    for hf in range(2):
                        te.matmul(pr[0:1, hf * 512:(hf + 1) * 512], onesb[:],
                                  sqb[:, hf * 512:(hf + 1) * 512],
                                  start=True, stop=True)
                bd.sec("tensor", f)
                tRM = bd.t("tensor")

                def f(ac, tRM=tRM):
                    ac.wait(tP, tRM)
                    ac.activation(out=lrow[:], in_=pr[0:1, 0:TPC], func=AF.Sqrt,
                                  bias=epsc[0:1, 0:1], scale=1.0)
                bd.sec("scalar", f)
                tLS = bd.t("scalar")

                def f(dv, tLS=tLS):
                    dv.wait(tA, tLS)
                    dv.reciprocal(out=lrow[:], in_=lrow[:])
                bd.sec("vector", f)
                tLR = bd.t("vector")

                def f(te, oc=oc, tLR=tLR):
                    te.wait(tV, tLR)
                    lhs = onesq1 if oc < 16 else ones1f
                    for hf in range(2):
                        te.matmul(pr[:, hf * 512:(hf + 1) * 512], lhs[0:1, 0:128],
                                  lrow[0:1, hf * 512:(hf + 1) * 512],
                                  start=True, stop=True)
                bd.sec("tensor", f)
                tLB = bd.t("tensor")

                def f(dv, oc=oc, tLB=tLB, slot=slot):
                    dv.wait(tP, tLB)
                    dv.wait(sT[slot], c.val(sT[slot]))
                    dv.tensor_mul(out=est[:, slot, 0:TPC], in0=acc[:], in1=pr[:, 0:TPC])
                bd.sec("vector", f)
                tQE = bd.t("vector")

                def f(gp, oc=oc, tQE=tQE, slot=slot):
                    gp.wait(tV, tQE)
                    gp.dma(qf_o[oc], est[:, slot, 0:TPC], sT[slot], c)
                bd.sec("gpsimd", f)
            else:
                def f(ac, tCV=tCV, slot=slot):
                    ac.wait(tV, tCV)
                    ac.wait(sT[slot], c.val(sT[slot]))
                    ac.activation(out=est[:, slot, 0:TPC], in_=acc[:], func=AF.Silu)
                bd.sec("scalar", f)
                tVS = bd.t("scalar")

                def f(gp, oc=oc, tVS=tVS, slot=slot):
                    gp.wait(tA, tVS)
                    gp.dma(v_o[oc - 32], est[:, slot, 0:TPC], sT[slot], c)
                bd.sec("gpsimd", f)
            j += 1

        for gi in range(16):
            i = 64 + gi
            emit_wdma(i)
            emit_wcast(i)
            emit_pmms(i, j)
            slot = gi % 2

            def f(ac, gi=gi, j=j, slot=slot):
                ac.wait(tP, eM[j])
                ac.wait(sT[slot], c.val(sT[slot]))
                ac.activation(out=est[:, slot, 0:TPC],
                              in_=pp[j % 2][:, 0:NH].rearrange(
                                  "p (t x) -> p t x", x=131)[:, :, 3:131],
                              func=AF.Silu)
            bd.sec("scalar", f)
            eE.append([("scalar", bd.t("scalar"))])
            tGS = bd.t("scalar")

            def f(gp, gi=gi, tGS=tGS, slot=slot):
                gp.wait(tA, tGS)
                gp.dma(sg_o[gi], est[:, slot, 0:TPC], sT[slot], c)
            bd.sec("gpsimd", f)
            j += 1

        i_ab = 80
        emit_wdma(i_ab)
        emit_wcast(i_ab)
        emit_pmms(i_ab, j)
        jab = j

        def f(dv):
            dv.wait(tP, eM[jab])
            dv.tensor_copy(out=abx[0:16, 0, :], in_=pp[jab % 2][0:16, 0:NH])
        bd.sec("vector", f)
        tAB = bd.t("vector")

        def f(ac):
            ac.wait(tP, eM[jab])
            ac.activation(out=bet[:, 0:NH], in_=pp[jab % 2][32:48, 0:NH],
                          func=AF.Sigmoid)
        bd.sec("scalar", f)
        tBE = bd.t("scalar")
        eE.append([("vector", tAB), ("scalar", tBE)])
        j += 1

        def f(dv):
            dv.wait(tA, tBE)
            dv.tensor_scalar(out=bet[:], in0=bet[:], scalar1=2.0, scalar2=None,
                             op0=ALU.mult)
        bd.sec("vector", f)
        tBE2 = bd.t("vector")

        def f(ac):
            ac.wait(tV, tAB)
            ac.activation(out=abx[0:16, 1, :], in_=abx[0:16, 0, :], func=AF.Abs,
                          bias=dtb[0:16, 0:1], scale=1.0)
            ac.sync()
            ac.activation(out=abx[0:16, 1, :], in_=abx[0:16, 1, :], func=AF.Exp,
                          bias=0.0, scale=-1.0)
            ac.sync()
            ac.activation(out=abx[0:16, 1, :], in_=abx[0:16, 1, :], func=AF.Ln,
                          bias=one16[0:16, 0:1], scale=1.0)
            ac.activation(out=ea[:], in_=alog[0:16, :], func=AF.Exp)
        bd.sec("scalar", f)
        tSP = bd.t("scalar")

        def f(dv):
            dv.wait(tA, tSP)
            dv.tensor_scalar(out=abx[0:16, 2, :], in0=abx[0:16, 0, :],
                             scalar1=dtb[0:16, 0:1], scalar2=0.0,
                             op0=ALU.add, op1=ALU.max)
            dv.sync()
            dv.tensor_add(out=abx[0:16, 2, :], in0=abx[0:16, 2, :],
                          in1=abx[0:16, 1, :])
            dv.sync()
            dv.tensor_scalar(out=ea[:], in0=ea[:], scalar1=-1.0, scalar2=None,
                             op0=ALU.mult)
            dv.sync()
            dv.tensor_scalar(out=gde[:], in0=abx[0:16, 2, :], scalar1=ea[0:16, 0:1],
                             scalar2=None, op0=ALU.mult)
        bd.sec("vector", f)
        tGD = bd.t("vector")

        def f(gp):
            gp.wait(tV, tBE2)
            gp.wait(tV, tGD)
            gp.dma(beta_o[:], bet[:].rearrange("p (t x) -> p t x", x=131)[:, :, 3:131],
                   sTP, c)
            gp.dma(gdec_o[:], gde[:].rearrange("p (t x) -> p t x", x=131)[:, :, 3:131],
                   sTP, c)
        bd.sec("gpsimd", f)

        phase_barrier()
        ctxC.close()
        ctxD = ExitStack()
        scd = lambda n, s, d: ctxD.enter_context(nc.sbuf_tensor(n, s, d))
        w2l = scd("w2l", [128, 2, 2048], F32)
        w2b = scd("w2b", [128, 2048], BF16)
        w2st = scd("w2st", [128, 2, 512], BF16)

        for wb in range(64):
            def f(sy, wb=wb):
                if wb >= 2:
                    sy.wait(tA, c.val(tA))
                sy.dma(w2l[:, wb % 2, :], w2r_i[:, wb * 2048:(wb + 1) * 2048],
                       sW2[wb % 2], c)
            bd.sec("sync", f)
            lw2 = c.val(sW2[wb % 2])

            def f(ac, wb=wb, lw2=lw2):
                ac.wait(sW2[wb % 2], lw2)
                ac.wait(tP, c.val(tP))
                ac.copy(out=w2b[:], in_=w2l[:, wb % 2, :])
            bd.sec("scalar", f)
            tW2C = bd.t("scalar")

            for qq in range(4):
                def f(te, wb=wb, qq=qq, tW2C=tW2C):
                    te.wait(tA, tW2C)
                    if qq == 0 and wb == 0:
                        te.wait(sLX, lX)
                    te.wait(tV, c.val(tV))
                    te.matmul(pr[:, 512:1024], h64c[:],
                              w2b[:, qq * 512:(qq + 1) * 512], start=True, stop=True)
                bd.sec("tensor", f)
                tW2M = bd.t("tensor")

                def f(dv, qq=qq, tW2M=tW2M):
                    dv.wait(tP, tW2M)
                    dv.wait(sPW, c.val(sPW))
                    dv.tensor_copy(out=w2st[:, qq % 2, :], in_=pr[:, 512:1024])
                bd.sec("vector", f)
                tW2E = bd.t("vector")

                def f(gp, wb=wb, qq=qq, tW2E=tW2E):
                    gp.wait(tV, tW2E)
                    jj = wb * 4 + qq
                    lo_w = jj // 4
                    dd0 = (jj % 4) * 4
                    for half in range(2):
                        lo = 64 * half + lo_w
                        dst = w2p[dd0:dd0 + 4, lo, :, :]
                        dst = dst.rearrange("e c d -> c e d")
                        srcv = w2st[64 * half:64 * half + 64, qq % 2, :]
                        srcv = srcv.rearrange("c (e d) -> c e d", d=128)
                        gp.dma(dst, srcv, sPW, c)
                bd.sec("gpsimd", f)
        wPD = c.val(sPW)

        phase_barrier()
        ctxD.close()
        ctxE = ExitStack()
        sce = lambda n, s, d: ctxE.enter_context(nc.sbuf_tensor(n, s, d))
        hblk = sce("hblk", [128, 64, 512], BF16)
        w2dd = sce("w2dd", [128, 32, 128], BF16)

        for tb in range(2):
            def xview(ch, tb=tb):
                return xn[:, ch, :].rearrange("p (t x) -> p t x", x=131)[
                    :, 4 * tb:4 * tb + 4, 3:131]

            for oc in range(64):
                i1 = 81 + 128 * tb + 2 * oc
                i3 = i1 + 1
                emit_wdma(i1)
                emit_wcast(i1)
                emit_wdma(i3)
                emit_wcast(i3)

                def f(te, oc=oc, i1=i1, i3=i3, tb=tb, xview=xview):
                    te.wait(tA, eC[i3])
                    te.wait(tV, c.val(tV))
                    for ch in range(NCH):
                        te.matmul(pr[:, 0:512], wbf[:, i1 % 2, ch, :],
                                  xview(ch), start=(ch == 0), stop=(ch == NCH - 1))
                    for ch in range(NCH):
                        te.matmul(pr[:, 512:1024], wbf[:, i3 % 2, ch, :],
                                  xview(ch), start=(ch == 0), stop=(ch == NCH - 1))
                bd.sec("tensor", f)
                tUP = bd.t("tensor")
                eM.append(tUP)
                eM.append(tUP)

                def f(ac, tUP=tUP):
                    ac.wait(tP, tUP)
                    ac.activation(out=cb[:, 0:512], in_=pr[:, 0:512], func=AF.Silu)
                bd.sec("scalar", f)
                tSU = bd.t("scalar")

                def f(dv, oc=oc, tSU=tSU):
                    dv.wait(tA, tSU)
                    dv.tensor_mul(out=hblk[:, oc, :], in0=pr[:, 512:1024],
                                  in1=cb[:, 0:512])
                bd.sec("vector", f)
            tHB = bd.t("vector")

            for oc in range(64):
                def f(te, oc=oc, tHB=tHB):
                    if oc == 0:
                        te.wait(tV, tHB)
                    else:
                        te.wait(tV, c.val(tV))
                    te.matmul(pr[:, 0:512], h128c[:], hblk[:, oc, :],
                              start=True, stop=True)
                bd.sec("tensor", f)
                tHM = bd.t("tensor")

                def f(dv, oc=oc, tHM=tHM):
                    dv.wait(tP, tHM)
                    dv.tensor_copy(out=hblk[:, oc, :], in_=pr[:, 0:512])
                bd.sec("vector", f)
            tHP = bd.t("vector")

            for dd in range(16):
                for hc in range(2):
                    def f(sy, dd=dd, tb=tb, hc=hc):
                        if not (tb == 0 and dd == 0 and hc == 0):
                            sy.wait(tP, c.val(tP))
                        sy.dma(w2dd[:], w2p[dd, :, 32 * hc:32 * hc + 32, :], sWD[0], c)
                    bd.sec("sync", f)
                    lwd = c.val(sWD[0])

                    def f(te, dd=dd, hc=hc, lwd=lwd, tHP=tHP):
                        te.wait(sWD[0], lwd)
                        te.wait(sPW, wPD)
                        te.wait(tV, tHP)
                        if hc == 0:
                            te.wait(tV, c.val(tV))
                        for cc2 in range(32):
                            te.matmul(pp0[:, 0:512], w2dd[:, cc2, :],
                                      hblk[:, 32 * hc + cc2, :],
                                      start=(hc == 0 and cc2 == 0),
                                      stop=(hc == 1 and cc2 == 31))
                    bd.sec("tensor", f)
                tW2P = bd.t("tensor")

                def f(dv, dd=dd, tb=tb, tW2P=tW2P):
                    dv.wait(tP, tW2P)
                    xv = x1[:, dd, :].rearrange("p (t x) -> p t x", x=131)[
                        :, 4 * tb:4 * tb + 4, 3:131]
                    dv.tensor_add(out=xv, in0=xv,
                                  in1=pp0[:, 0:512].rearrange("p (t x) -> p t x", x=128))
                bd.sec("vector", f)
        tFF = bd.t("vector")

        def f(gp):
            gp.wait(tV, tFF)
            for ch in range(NCH):
                gp.dma(pxb_o[ch],
                       x1[:, ch, :].rearrange("p (t x) -> p t x", x=131)[:, :, 3:131],
                       sTP, c)
        bd.sec("gpsimd", f)

        def f(gp):
            for s_ in sT:
                gp.wait(s_, c.val(s_))
            gp.wait(sTP, c.val(sTP))
            gp.wait(sPW, c.val(sPW))
        bd.sec("gpsimd", f)
        ctxE.close()
    return nc


def prep_launch3(inputs, l1res, l2res):
    def bf(a):
        a = np.asarray(a)
        if a.dtype.kind == 'V':
            a = a.view(BF)
        return a.astype(BF) if a.dtype != BF else a

    Wo = np.asarray(inputs["L0_attn_Wo"], np.float32)
    onw = np.asarray(inputs["L0_attn_o_norm_weight"], np.float32)
    Wqkv = np.asarray(inputs["L1_attn_W_qkv"], np.float32)
    Wab = np.asarray(inputs["L1_attn_W_ab"], np.float32)
    cw = np.asarray(inputs["L1_attn_conv_w"], np.float32)
    alog = np.asarray(inputs["L1_attn_A_log"], np.float32)
    dtb = np.asarray(inputs["L1_attn_dt_bias"], np.float32)
    W1 = np.asarray(inputs["L1_ffn_W1"], np.float32)
    W2 = np.asarray(inputs["L1_ffn_W2"], np.float32)
    W3 = np.asarray(inputs["L1_ffn_W3"], np.float32)

    wo = np.ascontiguousarray(Wo.T.reshape(NCH, 128, 2048))
    wqkv = np.ascontiguousarray(Wqkv.reshape(NCH, 128, 6144))
    wgt = np.ascontiguousarray(Wab[:, 32:2080].reshape(NCH, 128, 2048))
    wab = np.zeros((2048, 64), np.float32)
    wab[:, 0:16] = Wab[:, 0:16]
    wab[:, 32:48] = Wab[:, 16:32]
    wab = np.ascontiguousarray(wab.reshape(NCH, 128, 64))
    cwr = np.ascontiguousarray(cw.reshape(48, 128, 4).transpose(1, 0, 2))
    w1 = np.ascontiguousarray(W1.T.reshape(NCH, 128, 8192))
    w3 = np.ascontiguousarray(W3.T.reshape(NCH, 128, 8192))
    A = np.ascontiguousarray(W2.T).reshape(64, 262144)
    w2r = np.ascontiguousarray(np.concatenate([A[:, :131072], A[:, 131072:]], axis=0))
    h128 = (_hadamard(128) / np.sqrt(128.0)).astype(BF)
    h64 = _hadamard(64) / np.sqrt(64.0)
    h64bd = np.zeros((128, 128), np.float64)
    h64bd[:64, :64] = h64
    h64bd[64:, 64:] = h64
    h64bd = h64bd.astype(BF)
    onw2 = np.ascontiguousarray(np.tile(onw, 2)[:, None])

    in_maps = []
    for core in range(NCORE):
        b, par = core // 2, core % 2
        r0, r2_0 = l1res[2 * b], l2res[2 * b]
        r1, r2_1 = l1res[2 * b + 1], l2res[2 * b + 1]
        ofull = np.zeros((2048, S), BF)
        ofull[:, _positions(0)] = bf(r2_0["o"]).reshape(2048, TPC)
        ofull[:, _positions(1)] = bf(r2_1["o"]).reshape(2048, TPC)
        pxfull = np.zeros((2048, S), np.float32)
        pxfull[:, _positions(0)] = np.asarray(r0["pxa"], np.float32).reshape(2048, TPC)
        pxfull[:, _positions(1)] = np.asarray(r1["pxa"], np.float32).reshape(2048, TPC)

        def with_halo(full, dtype):
            out = np.zeros((2048, TCH), dtype)
            for ti in range(NT):
                g = 2 * ti + par
                c0 = g * 128
                out[:, ti * 131 + 3:(ti + 1) * 131] = full[:, c0:c0 + 128]
                if g > 0:
                    out[:, ti * 131:ti * 131 + 3] = full[:, c0 - 3:c0]
            return out

        oc_ = with_halo(ofull, BF).reshape(NCH, 128, TCH)
        pxc = with_halo(pxfull, np.float32).reshape(NCH, 128, TCH)
        in_maps.append({
            "o": np.ascontiguousarray(oc_), "pxa": np.ascontiguousarray(pxc),
            "onw": onw2, "wo": wo, "wqkv": wqkv, "wgt": wgt, "wab": wab,
            "cw": cwr, "alog": np.ascontiguousarray(alog[:, None]),
            "dtb": np.ascontiguousarray(dtb[:, None]),
            "w1": w1, "w3": w3, "w2r": w2r, "h128": h128, "h64": h64bd,
        })
    return in_maps



def build_launch4():
    nc = bass.Bass("TRN2", target_bir_lowering=False)
    dt = nc.dram_tensor
    qf_i = dt("qf", [8, 128, S], BF16, kind="ExternalInput").ap()
    kf_i = dt("kf", [8, 128, S], BF16, kind="ExternalInput").ap()
    kt_i = dt("kt", [8, 16, 128, 128], BF16, kind="ExternalInput").ap()
    vt_i = dt("vt", [8, 16, 128, 128], BF16, kind="ExternalInput").ap()
    st_i = dt("st", [8, 16, 128, 128], BF16, kind="ExternalInput").ap()
    g_i = dt("g", [8, 128, 16], F32, kind="ExternalInput").ap()
    be_i = dt("be", [8, 128, 16], F32, kind="ExternalInput").ap()
    lt_i = dt("lt", [128, 128], F32, kind="ExternalInput").ap()
    sfx_i = dt("sfx", [128, 128], F32, kind="ExternalInput").ap()
    mb_i = dt("mb", [128, 128], F32, kind="ExternalInput").ap()
    mbt_i = dt("mbt", [128, 128], F32, kind="ExternalInput").ap()
    idb_i = dt("idb", [128, 128], BF16, kind="ExternalInput").ap()
    idf_i = dt("idf", [128, 128], F32, kind="ExternalInput").ap()

    ot_o = dt("ot", [8, 16, 128, 128], BF16, kind="ExternalOutput").ap()

    with ExitStack() as ctx:
        sb = lambda n, s, d: ctx.enter_context(nc.sbuf_tensor(n, s, d))
        qf = sb("qf_s", [128, 2, S], BF16)
        kf = sb("kf_s", [128, 2, S], BF16)
        kt = sb("kt_s", [128, 2, 16, 128], BF16)
        vt = sb("vt_s", [128, 2, 16, 128], BF16)
        st = sb("st_s", [128, 2, 16, 128], BF16)
        gg = sb("gg_s", [128, 2, 16], F32)
        be = sb("be_s", [128, 2, 16], F32)
        nbe = sb("nbe", [128, 16], F32)
        Sf = sb("Sf", [128, 2, 128], F32)
        Sb = sb("Sb", [128, 2, 128], BF16)
        Psb = sb("Psb", [128, 2, 128], BF16)
        Ptsb = sb("Ptsb", [128, 2, 128], BF16)
        rbf = sb("rbf", [128, 128], BF16)
        rf32 = sb("rf32", [128, 128], F32)
        dbf = sb("dbf", [128, 128], BF16)
        Dx = sb("Dx", [128, 128], BF16)
        DxT = sb("DxT", [128, 128], BF16)
        Em = sb("Em", [128, 128], F32)
        Wsb = sb("Wsb", [128, 128], BF16)
        kw = sb("kw", [128, 128], BF16)
        dbet = sb("dbet", [128, 128], BF16)
        dbg = sb("dbg", [128, 128], BF16)
        o1s = sb("o1s", [128, 128], F32)
        SqkT = sb("SqkT", [128, 128], BF16)
        of = sb("of", [128, 128], F32)
        sqf = sb("sqf", [128, 128], F32)
        egam = sb("egam", [128, 1], F32)
        wde = sb("wde", [128, 1], F32)
        egc = sb("egc", [128, 1], F32)
        nbg = sb("nbg", [128, 1], F32)
        gsb = sb("gsb", [128, 1], F32)
        grow = sb("grow", [1, 128], F32)
        ngrow = sb("ngrow", [1, 128], F32)
        s2 = sb("s2", [128, 1], F32)
        rr = sb("rr", [128, 1], F32)
        oexp = sb("oexp", [128, 2, 128], BF16)
        ltc = sb("ltc", [128, 128], F32)
        sfxc = sb("sfxc", [128, 128], F32)
        mbc = sb("mbc", [128, 128], F32)
        mbtc = sb("mbtc", [128, 128], F32)
        identb = sb("identb_s", [128, 128], BF16)
        identf = sb("identf_s", [128, 128], F32)
        allc = sb("allc", [128, 128], F32)
        ones1 = sb("ones1", [1, 128], F32)
        eps5 = sb("eps5", [128, 1], F32)
        pX = ctx.enter_context(nc.psum_tensor("pX", [128, 512], F32))
        pY = ctx.enter_context(nc.psum_tensor("pY", [128, 512], F32))
        pZ = ctx.enter_context(nc.psum_tensor("pZ", [128, 512], F32))
        pR = ctx.enter_context(nc.psum_tensor("pR", [128, 512], F32))
        pTb = ctx.enter_context(nc.psum_tensor("pTb", [128, 128], BF16))

        bd = Bld(nc, ctx)
        c = bd.c
        tV, tA, tP, tG = (bd.tick[e] for e in ("vector", "scalar", "tensor", "gpsimd"))
        sLC = bd.sem(ctx, "sLC")
        sK = [bd.sem(ctx, "sK0"), bd.sem(ctx, "sK1")]
        sO = [bd.sem(ctx, "sO0"), bd.sem(ctx, "sO1")]

        def f(sy):
            sy.dma(ltc[:], lt_i, sLC, c)
            sy.dma(sfxc[:], sfx_i, sLC, c)
            sy.dma(mbc[:], mb_i, sLC, c)
            sy.dma(mbtc[:], mbt_i, sLC, c)
            sy.dma(identb[:], idb_i, sLC, c)
            sy.dma(identf[:], idf_i, sLC, c)
        bd.sec("sync", f)
        lC = c.val(sLC)

        def f(gp):
            gp.memset(allc[:], 1.0)
            gp.memset(ones1[:], 1.0)
            gp.memset(eps5[:], 1e-5)
        bd.sec("gpsimd", f)
        tOn = bd.t("gpsimd")

        kuse = []
        for bh in range(8):
            buf = bh % 2

            def f(sy, bh=bh, buf=buf):
                if bh >= 2:
                    for tk, vv in kuse[bh - 2]:
                        sy.wait(tk, vv)
                sy.dma(qf[:, buf, :], qf_i[bh], sK[buf], c)
                sy.dma(kf[:, buf, :], kf_i[bh], sK[buf], c)
                for ci in range(16):
                    sy.dma(kt[:, buf, ci, :], kt_i[bh, ci], sK[buf], c)
                    sy.dma(vt[:, buf, ci, :], vt_i[bh, ci], sK[buf], c)
                    sy.dma(st[:, buf, ci, :], st_i[bh, ci], sK[buf], c)
                sy.dma(gg[:, buf, :], g_i[bh], sK[buf], c)
                sy.dma(be[:, buf, :], be_i[bh], sK[buf], c)
            bd.sec("sync", f)
            kld = c.val(sK[buf])

            def f(dv, bh=bh, buf=buf, kld=kld):
                dv.wait(sK[buf], kld)
                dv.wait(tP, c.val(tP))
                dv.tensor_scalar(out=nbe[:], in0=be[:, buf, :], scalar1=-1.0,
                                 scalar2=None, op0=ALU.mult)
                dv.memset(Sf[:, 0, :], 0.0)
                dv.memset(Sb[:, 0, :], 0.0)
            bd.sec("vector", f)
            tPrep = bd.t("vector")

            for ci in range(16):
                cur, nxt = ci % 2, (ci + 1) % 2
                qs = qf[:, buf, ci * 128:(ci + 1) * 128]
                ks = kf[:, buf, ci * 128:(ci + 1) * 128]
                kts = kt[:, buf, ci, :]
                vts = vt[:, buf, ci, :]
                sts = st[:, buf, ci, :]

                def f(te, ci=ci, buf=buf):
                    if ci == 0:
                        te.wait(tV, tPrep)
                        te.wait(sLC, lC)
                        te.wait(tG, tOn)
                    te.wait(tV, c.val(tV))
                    te.wait(tA, c.val(tA))
                    te.matmul(pX[:, 0:1], ltc[:], gg[:, buf, ci:ci + 1],
                              start=True, stop=True)
                    te.matmul(pX[:, 1:2], sfxc[:], gg[:, buf, ci:ci + 1],
                              start=True, stop=True)
                    te.matmul(pX[:, 2:3], allc[:], gg[:, buf, ci:ci + 1],
                              start=True, stop=True)
                bd.sec("tensor", f)
                tGa = bd.t("tensor")

                def f(ac, tGa=tGa):
                    ac.wait(tP, tGa)
                    ac.activation(out=egam[:], in_=pX[:, 0:1], func=AF.Exp)
                    ac.activation(out=wde[:], in_=pX[:, 1:2], func=AF.Exp)
                    ac.activation(out=egc[:], in_=pX[:, 2:3], func=AF.Exp)
                bd.sec("scalar", f)
                tEg = bd.t("scalar")

                def f(dv, tGa=tGa):
                    dv.wait(tP, tGa)
                    dv.tensor_copy(out=gsb[:], in_=pX[:, 0:1])
                bd.sec("vector", f)
                tGs = bd.t("vector")

                def f(te, tGs=tGs, tEg=tEg):
                    te.wait(tV, tGs)
                    te.wait(tA, tEg)
                    te.transpose(pX[0:1, 384:512], gsb[:], identf[:])
                bd.sec("tensor", f)
                tGt = bd.t("tensor")

                def f(dv, tGt=tGt):
                    dv.wait(tP, tGt)
                    dv.tensor_copy(out=grow[:], in_=pX[0:1, 384:512])
                    dv.sync()
                    dv.tensor_scalar(out=ngrow[:], in0=grow[:], scalar1=-1.0,
                                     scalar2=None, op0=ALU.mult)
                bd.sec("vector", f)
                tGr = bd.t("vector")

                def f(te, tGr=tGr):
                    te.wait(tV, tGr)
                    te.matmul(pX[:, 128:256], grow[:], ones1[:], start=True, stop=False)
                    te.matmul(pX[:, 128:256], ones1[:], ngrow[:], start=False, stop=True)
                    te.matmul(pX[:, 256:384], ngrow[:], ones1[:], start=True, stop=False)
                    te.matmul(pX[:, 256:384], ones1[:], grow[:], start=False, stop=True)
                bd.sec("tensor", f)
                tE = bd.t("tensor")

                def f(dv, tE=tE):
                    dv.wait(tP, tE)
                    dv.tensor_add(out=Em[:], in0=pX[:, 128:256], in1=mbc[:])
                bd.sec("vector", f)
                tEm = bd.t("vector")

                def f(ac, tEm=tEm):
                    ac.wait(tV, tEm)
                    ac.activation(out=Dx[:], in_=Em[:], func=AF.Exp)
                bd.sec("scalar", f)
                tDx = bd.t("scalar")

                def f(dv, tDx=tDx):
                    dv.wait(tA, tDx)
                    dv.tensor_add(out=Em[:], in0=pX[:, 256:384], in1=mbtc[:])
                bd.sec("vector", f)
                tEm2 = bd.t("vector")

                def f(ac, tEm2=tEm2):
                    ac.wait(tV, tEm2)
                    ac.activation(out=DxT[:], in_=Em[:], func=AF.Exp)
                bd.sec("scalar", f)
                tDxT = bd.t("scalar")

                def f(te, ks=ks):
                    te.wait(tV, c.val(tV))
                    te.matmul(pY[:, 0:128], ks, ks, start=True, stop=True)
                bd.sec("tensor", f)
                tKK = bd.t("tensor")

                def f(dv, ci=ci, tKK=tKK, tDx=tDx, buf=buf):
                    dv.wait(tP, tKK)
                    dv.wait(tA, tDx)
                    dv.tensor_scalar(out=Psb[:, 0, :], in0=Dx[:],
                                     scalar1=nbe[:, ci:ci + 1], scalar2=None,
                                     op0=ALU.mult)
                    dv.sync()
                    dv.tensor_mul(out=Psb[:, 0, :], in0=Psb[:, 0, :], in1=pY[:, 0:128])
                bd.sec("vector", f)
                tP0 = bd.t("vector")

                def f(te, tP0=tP0):
                    te.wait(tV, tP0)
                    te.transpose(pTb[:], Psb[:, 0, :], identb[:])
                bd.sec("tensor", f)
                tPt = bd.t("tensor")

                def f(dv, tPt=tPt):
                    dv.wait(tP, tPt)
                    dv.tensor_copy(out=Ptsb[:, 0, :], in_=pTb[:])
                bd.sec("vector", f)
                tPt0 = bd.t("vector")

                def f(dv, ci=ci, buf=buf, tEg=tEg):
                    dv.wait(tA, tEg)
                    dv.tensor_scalar(out=dbet[:], in0=identb[:],
                                     scalar1=be[:, buf, ci:ci + 1], scalar2=None,
                                     op0=ALU.mult)
                    dv.sync()
                    dv.tensor_mul(out=nbg[:], in0=nbe[:, ci:ci + 1], in1=egam[:])
                    dv.sync()
                    dv.tensor_scalar(out=dbg[:], in0=identb[:], scalar1=nbg[:, 0:1],
                                     scalar2=None, op0=ALU.mult)
                bd.sec("vector", f)
                tDg = bd.t("vector")

                def f(te, ks=ks, cur=cur, tDg=tDg):
                    te.wait(tV, tDg)
                    te.matmul(pY[:, 128:256], ks, Sb[:, cur, :], start=True, stop=True)
                bd.sec("tensor", f)
                tW = bd.t("tensor")

                def f(dv, tW=tW):
                    dv.wait(tP, tW)
                    dv.tensor_copy(out=Wsb[:], in_=pY[:, 128:256])
                bd.sec("vector", f)
                tWs = bd.t("vector")

                def f(te, vts=vts, tWs=tWs):
                    te.wait(tV, tWs)
                    te.matmul(pR[:, 0:128], dbet[:], vts, start=True, stop=False)
                    te.matmul(pR[:, 0:128], dbg[:], Wsb[:], start=False, stop=True)
                bd.sec("tensor", f)
                tR0 = bd.t("tensor")

                def f(dv, tR0=tR0):
                    dv.wait(tP, tR0)
                    dv.tensor_copy(out=rf32[:], in_=pR[:, 0:128])
                    dv.tensor_copy(out=rbf[:], in_=pR[:, 0:128])
                bd.sec("vector", f)
                tRb = bd.t("vector")

                for kk2 in range(7):
                    def f(te, kk2=kk2, tRb=tRb):
                        te.wait(tV, tRb)
                        te.matmul(pR[:, 128:256], Ptsb[:, kk2 % 2, :], rbf[:],
                                  start=True, stop=True)
                        if kk2 < 6:
                            te.matmul(pZ[:, 0:128], Ptsb[:, kk2 % 2, :],
                                      Psb[:, kk2 % 2, :], start=True, stop=True)
                            te.matmul(pZ[:, 128:256], Psb[:, kk2 % 2, :],
                                      Ptsb[:, kk2 % 2, :], start=True, stop=True)
                    bd.sec("tensor", f)
                    tApply = bd.t("tensor")

                    def f(dv, kk2=kk2, tApply=tApply):
                        dv.wait(tP, tApply)
                        dv.tensor_add(out=rf32[:], in0=rf32[:], in1=pR[:, 128:256])
                        dv.sync()
                        dv.tensor_copy(out=rbf[:], in_=rf32[:])
                        if kk2 < 6:
                            dv.tensor_copy(out=Psb[:, (kk2 + 1) % 2, :],
                                           in_=pZ[:, 0:128])
                            dv.tensor_copy(out=Ptsb[:, (kk2 + 1) % 2, :],
                                           in_=pZ[:, 128:256])
                    bd.sec("vector", f)
                    tRb = bd.t("vector")

                def f(dv):
                    dv.sync()
                    dv.tensor_copy(out=dbf[:], in_=rf32[:])
                bd.sec("vector", f)
                tDel = bd.t("vector")

                def f(dv, kts=kts, tDel=tDel):
                    dv.sync()
                    dv.tensor_scalar(out=kw[:], in0=kts, scalar1=wde[:, 0:1],
                                     scalar2=None, op0=ALU.mult)
                bd.sec("vector", f)
                tKw = bd.t("vector")

                def f(te, tKw=tKw, tDel=tDel):
                    te.wait(tV, tKw)
                    te.wait(tV, tDel)
                    te.matmul(pZ[:, 256:384], kw[:], dbf[:], start=True, stop=True)
                bd.sec("tensor", f)
                tSp = bd.t("tensor")

                def f(dv, cur=cur, nxt=nxt, tSp=tSp):
                    dv.wait(tP, tSp)
                    dv.scalar_tensor_tensor(out=Sf[:, nxt, :], in0=Sf[:, cur, :],
                                            scalar=egc[:, 0:1], in1=pZ[:, 256:384],
                                            op0=ALU.mult, op1=ALU.add)
                    dv.sync()
                    dv.tensor_copy(out=Sb[:, nxt, :], in_=Sf[:, nxt, :])
                bd.sec("vector", f)
                tSn = bd.t("vector")

                def f(te, qs=qs, cur=cur):
                    te.wait(tV, c.val(tV))
                    te.matmul(pZ[:, 384:512], qs, Sb[:, cur, :], start=True, stop=True)
                bd.sec("tensor", f)
                tO1 = bd.t("tensor")

                def f(dv, tO1=tO1):
                    dv.wait(tP, tO1)
                    dv.tensor_scalar(out=o1s[:], in0=pZ[:, 384:512],
                                     scalar1=egam[:, 0:1], scalar2=None, op0=ALU.mult)
                bd.sec("vector", f)
                tO1s = bd.t("vector")

                def f(te, ks=ks, qs=qs, tO1s=tO1s):
                    te.wait(tV, tO1s)
                    te.matmul(pY[:, 256:384], ks, qs, start=True, stop=True)
                bd.sec("tensor", f)
                tQK = bd.t("tensor")

                def f(dv, tQK=tQK, tDxT=tDxT):
                    dv.wait(tP, tQK)
                    dv.wait(tA, tDxT)
                    dv.tensor_mul(out=SqkT[:], in0=pY[:, 256:384], in1=DxT[:])
                bd.sec("vector", f)
                tSq = bd.t("vector")

                def f(te, tSq=tSq, tDel=tDel):
                    te.wait(tV, tSq)
                    te.matmul(pY[:, 128:256], SqkT[:], dbf[:], start=True, stop=True)
                bd.sec("tensor", f)
                tO2 = bd.t("tensor")

                def f(dv, sts=sts, tO2=tO2):
                    dv.wait(tP, tO2)
                    dv.tensor_add(out=of[:], in0=o1s[:], in1=pY[:, 128:256])
                    dv.sync()
                    dv.tensor_mul(out=of[:], in0=of[:], in1=sts)
                bd.sec("vector", f)
                tOf = bd.t("vector")

                def f(ac, tOf=tOf):
                    ac.wait(tV, tOf)
                    ac.activation(out=sqf[:], in_=of[:], func=AF.Square)
                bd.sec("scalar", f)
                tSqf = bd.t("scalar")

                def f(dv, tSqf=tSqf):
                    dv.wait(tA, tSqf)
                    dv.tensor_reduce(out=s2[:], in_=sqf[:], axis=AX.X, op=ALU.add)
                bd.sec("vector", f)
                tS2 = bd.t("vector")

                def f(ac, tS2=tS2):
                    ac.wait(tV, tS2)
                    ac.activation(out=rr[:], in_=s2[:], func=AF.Sqrt,
                                  bias=eps5[:, 0:1], scale=1.0 / 128.0)
                bd.sec("scalar", f)
                tRr = bd.t("scalar")

                slot = ci % 2

                def f(dv, slot=slot, tRr=tRr):
                    dv.wait(tA, tRr)
                    dv.reciprocal(out=rr[:], in_=rr[:])
                    dv.sync()
                    dv.wait(sO[slot], c.val(sO[slot]))
                    dv.tensor_scalar(out=oexp[:, slot, :], in0=of[:],
                                     scalar1=rr[:, 0:1], scalar2=None, op0=ALU.mult)
                bd.sec("vector", f)
                tOe = bd.t("vector")

                def f(gp, bh=bh, ci=ci, slot=slot, tOe=tOe):
                    gp.wait(tV, tOe)
                    gp.dma(ot_o[bh, ci], oexp[:, slot, :], sO[slot], c)
                bd.sec("gpsimd", f)

            kuse.append([(tP, bd.t("tensor")), (tV, bd.t("vector")),
                         (tA, bd.t("scalar"))])

        def f(gp):
            gp.wait(sO[0], c.val(sO[0]))
            gp.wait(sO[1], c.val(sO[1]))
        bd.sec("gpsimd", f)
    return nc


def prep_launch4(l3res):
    def bf(a):
        a = np.asarray(a)
        if a.dtype.kind == 'V':
            a = a.view(BF)
        return a.astype(BF) if a.dtype != BF else a

    def f32(a):
        return np.asarray(a, np.float32)

    n = 128
    i_ = np.arange(n)[:, None]
    j_ = np.arange(n)[None, :]
    lt = (i_ <= j_).astype(np.float32)
    sfx = (i_ > j_).astype(np.float32)
    mb = np.where(i_ > j_, 0.0, NEG).astype(np.float32)
    mbt = np.where(j_ >= i_, 0.0, NEG).astype(np.float32)
    idb = np.eye(n, dtype=BF)
    idf = np.eye(n, dtype=np.float32)

    in_maps = []
    for core in range(NCORE):
        b, hb = core // 2, core % 2
        r0, r1 = l3res[2 * b], l3res[2 * b + 1]
        qk = np.zeros((32, 128, S), BF)
        qk[:, :, _positions(0)] = bf(r0["qf"])
        qk[:, :, _positions(1)] = bf(r1["qf"])
        vfull = np.zeros((16, 128, S), BF)
        vfull[:, :, _positions(0)] = bf(r0["vv"])
        vfull[:, :, _positions(1)] = bf(r1["vv"])
        sgf = np.zeros((16, 128, S), BF)
        sgf[:, :, _positions(0)] = bf(r0["sg"])
        sgf[:, :, _positions(1)] = bf(r1["sg"])
        bet = np.zeros((16, S), np.float32)
        bet[:, _positions(0)] = f32(r0["beta"])
        bet[:, _positions(1)] = f32(r1["beta"])
        gde = np.zeros((16, S), np.float32)
        gde[:, _positions(0)] = f32(r0["gdec"])
        gde[:, _positions(1)] = f32(r1["gdec"])

        heads = range(8 * hb, 8 * hb + 8)
        qfh = np.stack([qk[h] for h in heads])
        kfh = np.stack([qk[16 + h] for h in heads])
        kth = np.stack([np.ascontiguousarray(qk[16 + h].T).reshape(16, 128, 128)
                        for h in heads])
        vth = np.stack([np.ascontiguousarray(vfull[h].T).reshape(16, 128, 128)
                        for h in heads])
        sth = np.stack([np.ascontiguousarray(sgf[h].T).reshape(16, 128, 128)
                        for h in heads])
        gh = np.stack([np.ascontiguousarray(gde[h].reshape(16, 128).T)
                       for h in heads])
        beh = np.stack([np.ascontiguousarray(bet[h].reshape(16, 128).T)
                        for h in heads])
        in_maps.append({
            "qf": qfh, "kf": kfh, "kt": kth, "vt": vth, "st": sth,
            "g": gh, "be": beh, "lt": lt, "sfx": sfx, "mb": mb, "mbt": mbt,
            "idb": idb, "idf": idf,
        })
    return in_maps



def build_launch5():
    nc = bass.Bass("TRN2", target_bir_lowering=False)
    dt = nc.dram_tensor
    o1_i = dt("o1", [NCH, 128, TPC], BF16, kind="ExternalInput").ap()
    pxb_i = dt("pxb", [NCH, 128, TPC], F32, kind="ExternalInput").ap()
    wo1_i = dt("wo1", [NCH, 128, 2048], F32, kind="ExternalInput").ap()
    onw_i = dt("onw", [128, 1], F32, kind="ExternalInput").ap()
    out_o = dt("out", [NCH, 128, TPC], F32, kind="ExternalOutput").ap()

    with ExitStack() as ctx:
        sb = lambda n, s, d: ctx.enter_context(nc.sbuf_tensor(n, s, d))
        ob = sb("ob", [128, NCH, TPC], BF16)
        xr = sb("xr", [128, NCH, TPC], F32)
        wl = sb("wl", [128, 2, NCH, 128], F32)
        wbf = sb("wbf", [128, 2, NCH, 128], BF16)
        onw = sb("onw_s", [128, 1], F32)
        pp0 = ctx.enter_context(nc.psum_tensor("pp0", [128, TPC], F32))
        pp1 = ctx.enter_context(nc.psum_tensor("pp1", [128, TPC], F32))
        pp = [pp0, pp1]

        bd = Bld(nc, ctx)
        c = bd.c
        tV, tA, tP, tG = (bd.tick[e] for e in ("vector", "scalar", "tensor", "gpsimd"))
        sLX = bd.sem(ctx, "sLX")
        sWp = [bd.sem(ctx, "sW0"), bd.sem(ctx, "sW1")]
        sTP = bd.sem(ctx, "sTP")

        def f(sy):
            for ch in range(NCH):
                sy.dma(ob[:, ch, :], o1_i[ch], sLX, c)
                sy.dma(xr[:, ch, :], pxb_i[ch], sLX, c)
            sy.dma(onw[:], onw_i, sLX, c)
        bd.sec("sync", f)
        lX = c.val(sLX)

        eW, eC, eM, eE = [], [], [], []
        for i in range(16):
            def f(sy, i=i):
                if i >= 2:
                    sy.wait(tA, eC[i - 2])
                for ch in range(NCH):
                    sy.dma(wl[:, i % 2, ch, :], wo1_i[ch, :, i * 128:(i + 1) * 128],
                           sWp[i % 2], c)
            bd.sec("sync", f)
            eW.append(c.val(sWp[i % 2]))

            def f(ac, i=i):
                ac.wait(sWp[i % 2], eW[i])
                if i == 0:
                    ac.wait(sLX, lX)
                if i >= 2:
                    ac.wait(tP, eM[i - 2])
                for ch in range(NCH):
                    ac.activation(out=wbf[:, i % 2, ch, :], in_=wl[:, i % 2, ch, :],
                                  func=AF.Copy, bias=0.0, scale=onw[:, 0:1])
            bd.sec("scalar", f)
            eC.append(bd.t("scalar"))

            def f(te, i=i):
                te.wait(tA, eC[i])
                if i >= 2:
                    te.wait(tV, eE[i - 2])
                for ch in range(NCH):
                    for h in range(2):
                        te.matmul(pp[i % 2][:, h * 512:(h + 1) * 512],
                                  wbf[:, i % 2, ch, :],
                                  ob[:, ch, h * 512:(h + 1) * 512],
                                  start=(ch == 0), stop=(ch == NCH - 1))
            bd.sec("tensor", f)
            eM.append(bd.t("tensor"))

            def f(dv, i=i):
                dv.wait(tP, eM[i])
                dv.tensor_add(out=xr[:, i, :], in0=xr[:, i, :], in1=pp[i % 2][:])
            bd.sec("vector", f)
            eE.append(bd.t("vector"))
            tXA = bd.t("vector")

            def f(gp, i=i, tXA=tXA):
                gp.wait(tV, tXA)
                gp.dma(out_o[i], xr[:, i, :], sTP, c)
            bd.sec("gpsimd", f)

        def f(gp):
            gp.wait(sTP, c.val(sTP))
        bd.sec("gpsimd", f)
    return nc


def prep_launch5(inputs, l3res, l4res):
    def bf(a):
        a = np.asarray(a)
        if a.dtype.kind == 'V':
            a = a.view(BF)
        return a.astype(BF) if a.dtype != BF else a

    W_o = np.asarray(inputs["L1_attn_W_o"], np.float32)
    onw = np.asarray(inputs["L1_attn_o_norm_w"], np.float32)
    wo1 = np.ascontiguousarray(W_o.reshape(NCH, 128, 2048))
    onwc = np.ascontiguousarray(onw[:, None])

    in_maps = []
    for core in range(NCORE):
        b, par = core // 2, core % 2
        ofm = np.zeros((2048, S), BF)
        for hb in range(2):
            ot = bf(l4res[2 * b + hb]["ot"])
            for i in range(8):
                h = 8 * hb + i
                ofm[128 * h:128 * (h + 1), :] = ot[i].reshape(2048, 128).T
        idx = _positions(par)
        o1 = np.ascontiguousarray(ofm[:, idx].reshape(NCH, 128, TPC))
        in_maps.append({
            "o1": o1,
            "pxb": np.ascontiguousarray(np.asarray(l3res[core]["pxb"], np.float32)),
            "wo1": wo1, "onw": onwc,
        })
    return in_maps


def kernel(**inputs):
    ncs = [build_launch1(), build_launch2(), build_launch3(),
           build_launch4(), build_launch5()]
    cores = list(range(NCORE))

    im1 = prep_launch1(inputs)
    r1 = run_bass_kernel_spmd(ncs[0], im1, core_ids=cores).results
    im2 = prep_launch2(r1)
    r2 = run_bass_kernel_spmd(ncs[1], im2, core_ids=cores).results
    im3 = prep_launch3(inputs, r1, r2)
    r3 = run_bass_kernel_spmd(ncs[2], im3, core_ids=cores).results
    im4 = prep_launch4(r3)
    r4 = run_bass_kernel_spmd(ncs[3], im4, core_ids=cores).results
    im5 = prep_launch5(inputs, r3, r4)
    r5 = run_bass_kernel_spmd(ncs[4], im5, core_ids=cores).results

    out = np.zeros((B, S, D), np.float32)
    for core in range(NCORE):
        b, par = core // 2, core % 2
        o = np.asarray(r5[core]["out"], np.float32).reshape(D, TPC)
        out[b][_positions(par)] = o.T
    return out
